# version 08
# baseline (speedup 1.0000x reference)
import numpy as np
import ml_dtypes
import concourse.bass as bass
import concourse.mybir as mybir
from concourse.bass_utils import run_bass_kernel_spmd

F32 = mybir.dt.float32
BF16 = mybir.dt.bfloat16
AF = mybir.ActivationFunctionType
ALU = mybir.AluOpType
AXX = mybir.AxisListType.X

D = 2048
DI = 4096
NH = 64
HD = 64
CK = 4
PROJ = 8512
INTER = 5632
TG = 512
NTI = 4
NCH = 4
NGRP = 8
SEQLEN = 2048
NCORE = 8
NKD = 16
NKI = 32
NJG = 32
NJX = 34
NJF = 44
EXP_CLAMP = 80.0

CONV_ORDER = [32, 33] + list(range(32))

DEBUG_TAPS = []


def _bf(a):
    return np.asarray(a).astype(ml_dtypes.bfloat16)


ENGS = ("sync", "tensor", "vector", "scalar", "gpsimd")


class Ctx:
    def __init__(self, nc):
        self.nc = nc
        self.q = {e: [] for e in ENGS}
        self.sem_cm = []
        self.sems = {}
        self.cnt = {}

    def sem(self, name):
        cm = self.nc.semaphore(name)
        self.sems[name] = cm.__enter__()
        self.sem_cm.append(cm)
        self.cnt[name] = 0

    def emit(self, eng, fn):
        self.q[eng].append(fn)

    def wait(self, eng, sname, val):
        if val <= 0:
            return
        h = self.sems[sname]
        self.emit(eng, lambda e, h=h, v=val: e.wait_ge(h, v))

    def wait_evt(self, eng, evt):
        if evt is not None:
            self.wait(eng, evt[0], evt[1])

    def dma(self, eng, out, in_, sname):
        self.cnt[sname] += 16
        tgt = self.cnt[sname]
        h = self.sems[sname]
        self.emit(eng, lambda e, o=out, i=in_, h=h: e.dma_start(out=o, in_=i).then_inc(h, 16))
        return (sname, tgt)

    def dma_tp(self, eng, out, in_, sname):
        self.cnt[sname] += 16
        tgt = self.cnt[sname]
        h = self.sems[sname]
        self.emit(eng, lambda e, o=out, i=in_, h=h: e.dma_start_transpose(o, i).then_inc(h, 16))
        return (sname, tgt)

    def op(self, eng, builder, inc=None, drain=None):
        if inc is None:
            self.emit(eng, builder)
            return None
        self.cnt[inc] += 1
        tgt = self.cnt[inc]
        h = self.sems[inc]
        if drain is None:
            drain = False
        if not drain:
            self.emit(eng, lambda e, b=builder, h=h: b(e).then_inc(h, 1))
        else:
            self.emit(eng, lambda e, b=builder, h=h: (b(e), e.drain().then_inc(h, 1)))
        return (inc, tgt)

    def drain(self, eng):
        self.emit(eng, lambda e: e.drain())

    def now(self, sname):
        return (sname, self.cnt[sname])


def build_program():
    nc = bass.Bass("TRN2", target_bir_lowering=False)
    c = Ctx(nc)

    def din(name, shape, dt=F32):
        return nc.dram_tensor(name, list(shape), dt, kind="ExternalInput").ap()

    x_in = din("x", (2, SEQLEN, D))
    wl_ip = din("wl_ip", (34, 128, 4096))
    wl_m4 = din("wl_m4", (16, 128, 4096))
    wl_f1 = din("wl_f1", (44, 128, 4096))
    wl_f2 = din("wl_f2", (16, 128, 5632))
    wl_cv = din("wl_cv", (34, 128, 512))
    conv_bl = din("conv_bl", (128, 34))
    a_log = din("a_log", (64, 1))
    dt_bias_i = din("dt_bias", (64, 1))
    d_bc = din("d_bc", (128, 64))
    norm_wl = din("norm_wl", (128, 32))
    gs_l = din("gs_l", (128, 44))
    us_l = din("us_l", (128, 44))
    ds_l = din("ds_l", (128, 16))
    ident_bf_i = din("ident_bf", (128, 128), BF16)
    ident_f_i = din("ident_f", (128, 128), F32)
    trimask_i = din("trimask", (128, 128), F32)
    ones_t_i = din("ones_t", (128, 128), F32)
    ones_r_i = din("ones_r", (1, 128), F32)
    misc_i = din("misc_l", (128, 2), F32)

    out = nc.dram_tensor("o", [2, SEQLEN, D], F32, kind="ExternalOutput").ap()

    def scr(name, shape, dt=BF16):
        return nc.dram_tensor(name, list(shape), dt, kind="Internal").ap()

    scr_ip = scr("scr_ip", (34, 128, 4096))
    scr_conv = scr("scr_conv", (34, 128, 512))
    scr_m4 = scr("scr_m4", (16, 128, 4096))
    scr_f1 = scr("scr_f1", (44, 128, 4096))
    scr_f2 = scr("scr_f2", (16, 128, 5632))

    taps = {}
    for (tname, kind, tg) in DEBUG_TAPS:
        shp = {"x_tm": [128, NCH * DI], "y_tm": [128, NCH * DI], "x1": [128, NTI * D],
               "xbc": [128, NJX * 515], "acs": [64, 3 * TG],
               "xn_fm": [128, NKD * TG], "scr_ip": [34, 128, 4096],
               "ln1dbg": [128, 4 * D + 64], "sbf": [128, NH * HD],
               "yn_fm": [128, NKI * TG], "mo_tm": [128, NTI * D]}[kind]
        taps[tname] = nc.dram_tensor(tname, shp, F32, kind="ExternalOutput").ap()

    def sb_at(name, shape, dt, off):
        return nc.alloc_sbuf_tensor_at(name, list(shape), dt, offset=off + 16512)

    ident_bf = sb_at("ident_bf_s", [128, 128], BF16, 0)
    ident_f = sb_at("ident_f_s", [128, 128], F32, 256)
    trimask = sb_at("trimask_s", [128, 128], F32, 768)
    ones_t = sb_at("ones_t_s", [128, 128], F32, 1280)
    ones_r = sb_at("ones_r_s", [1, 128], F32, 1792)
    convb = sb_at("convb_s", [128, 34], F32, 2848)
    nega = sb_at("nega_s", [64, 1], F32, 3008)
    dtb = sb_at("dtb_s", [64, 1], F32, 3040)
    dbc = sb_at("dbc_s", [128, 64], F32, 3072)
    normw = sb_at("normw_s", [128, 32], F32, 3328)
    gs_c = sb_at("gs_s", [128, 44], F32, 3456)
    us_c = sb_at("us_s", [128, 44], F32, 3648)
    ds_c = sb_at("ds_s", [128, 16], F32, 3840)
    stat = sb_at("stat_s", [128, 24], F32, 211968)
    row_sb = sb_at("row_s", [1, 64], F32, 212064)
    di_sb = sb_at("di_s", [128, 256], BF16, 212320)
    misc_c = sb_at("misc_s", [128, 2], F32, 3904)
    xreg = sb_at("xreg", [128, NTI * D], BF16, 4096)
    xn_tmp = sb_at("xn_tmp", [128, D], BF16, 20480)
    xn_fm = sb_at("xn_fm", [128, NKD * TG], BF16, 24576)
    mo_tm = sb_at("mo_tm", [128, NTI * D], BF16, 20480)
    gtmp = sb_at("gtmp", [128, 2 * TG], BF16, 40960)
    sg_tm = sb_at("sg_tm", [128, NTI * DI], BF16, 43008)
    mo_fm = sb_at("mo_fm", [128, 16 * TG], BF16, 43008)
    ffn_fm = sb_at("ffn_fm", [128, 16 * TG], BF16, 43008)
    tsil = sb_at("tsil", [128, TG], F32, 59392)
    xbc = sb_at("xbc", [128, NJX * 515], BF16, 75776)
    ssd_mats = sb_at("ssd_mats", [128, NCH * 6 * 64], F32, 75776)
    dmin4 = sb_at("dmin4", [128, 2 * TG], F32, 81920)
    l24 = sb_at("l24", [128, 2 * TG], BF16, 86016)
    lhy4 = sb_at("lhy4", [128, 2 * TG], BF16, 88064)
    gcat = sb_at("gcat", [128, TG], BF16, 90112)
    t4cat = sb_at("t4cat", [128, 512], F32, 91136)
    xp8 = sb_at("xp8", [128, 512], BF16, 93184)
    m1t_sb = sb_at("m1t_sb", [128, 8 * 128], F32, 94208)
    btm_sb = sb_at("btm_sb", [128, NCH * 128], BF16, 98304)
    yn_fm = sb_at("yn_fm", [128, NKI * TG], BF16, 75776)
    x2_scr = sb_at("x2_scr", [128, 2 * D], F32, 75776)
    x_tm = sb_at("x_tm", [128, NCH * DI], BF16, 111616)
    hdn = sb_at("hdn", [128, NJF * TG], BF16, 111616)
    convsilu = sb_at("convsilu", [128, 2 * TG], BF16, 144384)
    bc_fm = sb_at("bc_fm", [128, 2 * TG], BF16, 146432)
    diag_sb = sb_at("diag_sb", [128, 2 * 7 * TG], BF16, 148480)
    sqd = sb_at("sqd", [128, DI], BF16, 148480)
    nm_scr = sb_at("nm_scr", [128, D], F32, 162816)
    s_bf = sb_at("s_bf", [128, NH * HD], BF16, 171008)
    dt_z = sb_at("dt_z", [64, TG], F32, 179200)
    dt_t1 = sb_at("dt_t1", [64, TG], F32, 181248)
    dt_t2 = sb_at("dt_t2", [64, TG], F32, 183296)
    acs_fm = sb_at("acs_fm", [64, TG], F32, 185344)
    carry_sb = sb_at("carry_s", [128, 34 * 3], BF16, 181248)
    lndt_fm = sb_at("lndt_fm", [64, TG], F32, 187392)
    wsb = [sb_at(f"wsb{i}", [128, 5632], BF16, 189440 + i * 11264) for i in range(2)]

    ps_gemm = [nc.alloc_psum_tensor(f"ps_gemm{i}", [128, 512], F32) for i in range(2)]
    ps_m1t = [nc.alloc_psum_tensor(f"ps_m1t{i}", [128, 512], F32) for i in range(2)]
    ps_yo = [nc.alloc_psum_tensor(f"ps_yo{i}", [128, 512], F32) for i in range(2)]
    ps_s = [nc.alloc_psum_tensor(f"ps_s{i}", [128, 512], F32) for i in range(2)]
    ps_gemm_bf = [p[:].bitcast(BF16) for p in ps_gemm]

    for s in ("qsy", "qgp", "qsc", "cpe", "cdv", "cac", "cpo"):
        c.sem(s)

    x0_evts = []
    for t in range(NTI):
        x0_evts.append(c.dma("gpsimd", xreg[:, t * D:(t + 1) * D],
                             x_in[0, t * 128:(t + 1) * 128, :], "qgp"))

    const_last = None
    for dst, srcap in ((ident_bf, ident_bf_i), (ident_f, ident_f_i), (trimask, trimask_i),
                       (ones_t, ones_t_i), (ones_r, ones_r_i),
                       (convb, conv_bl), (dbc, d_bc), (normw, norm_wl), (gs_c, gs_l),
                       (us_c, us_l), (ds_c, ds_l), (nega, a_log), (dtb, dt_bias_i),
                       (misc_c, misc_i)):
        const_last = c.dma("sync", dst[:], srcap, "qsy")
    c.wait_evt("scalar", const_last)
    c.op("scalar", lambda e: e.activation(nega[:], nega[:], AF.Exp))
    nega_evt = c.op("scalar", lambda e: e.mul(nega[:], nega[:], -1.0), inc="cac")

    def cast_lay(srcap, dst, nslab, per):
        ev = None
        for s0 in range(0, nslab, per):
            s1 = min(s0 + per, nslab)
            ev = c.dma("gpsimd", dst[s0:s1], srcap[s0:s1], "qgp")
        return ev

    prep_ip = cast_lay(wl_ip, scr_ip, 34, 3)
    prep_conv_done = cast_lay(wl_cv, scr_conv, 34, 17)
    prep_m4 = cast_lay(wl_m4, scr_m4, 16, 3)
    prep_f1 = cast_lay(wl_f1, scr_f1, 44, 3)
    prep_f2 = cast_lay(wl_f2, scr_f2, 16, 2)
    prep_all = c.now("qgp")

    ring = {"idx": 0, "evac": [None, None]}

    def ring_next():
        i = ring["idx"] % 2
        ev = ring["evac"][i]
        ring["idx"] += 1
        return i, ev

    def ring_evac(i, evt):
        ring["evac"][i] = evt

    strip = {"ctr": 0, "use": [None, None]}

    def load_strip(src_ap, width, gate=None):
        b = strip["ctr"] % 2
        strip["ctr"] += 1
        c.wait_evt("sync", strip["use"][b])
        if gate is not None:
            c.wait("sync", gate[0], gate[1])
        ld = c.dma("sync", wsb[b][:, 0:width], src_ap, "qsy")
        return b, ld

    diag_st = {"ctr": 0, "use": [None, None], "ld": [None, None]}
    prev_group_store = [None]

    ln_state = {"redsq": None}

    def ln_stats(src, t, sq_dump, src_evt=None):
        width = src.shape[-1]
        c.wait_evt("vector", src_evt)
        c.wait_evt("scalar", src_evt)
        c.op("vector", lambda e: e.reduce_sum(stat[:, 8 + t:9 + t], src, axis=AXX))
        sq_ev = c.op("scalar", lambda e: e.activation(
            sq_dump, src, AF.Square, accum_out=stat[:, 12 + t:13 + t]), inc="cac",
            drain=True)
        c.wait_evt("vector", sq_ev)
        c.drain("vector")
        c.op("vector", lambda e: e.tensor_scalar(
            stat[:, 0 + t:1 + t], stat[:, 8 + t:9 + t], 1.0 / width, None, ALU.mult))
        c.op("vector", lambda e: e.tensor_scalar(
            stat[:, 12 + t:13 + t], stat[:, 12 + t:13 + t], 1.0 / width, None, ALU.mult))
        c.drain("vector")
        c.op("vector", lambda e: e.tensor_tensor(
            stat[:, 16 + t:17 + t], stat[:, 0 + t:1 + t], stat[:, 0 + t:1 + t], ALU.mult))
        c.drain("vector")
        var_ev = c.op("vector", lambda e: e.tensor_tensor(
            stat[:, 12 + t:13 + t], stat[:, 12 + t:13 + t], stat[:, 16 + t:17 + t],
            ALU.subtract), inc="cdv")
        c.wait_evt("scalar", var_ev)
        c.op("scalar", lambda e: e.activation(
            stat[:, 4 + t:5 + t], stat[:, 12 + t:13 + t], AF.Ln, bias=misc_c[:, 0:1]))
        c.drain("scalar")
        rstd_ev = c.op("scalar", lambda e: e.activation(
            stat[:, 4 + t:5 + t], stat[:, 4 + t:5 + t], AF.Exp, scale=-0.5), inc="cac",
            drain=True)
        return rstd_ev

    xtm_r = x_tm[:].rearrange("p (ch q) -> p ch q", q=DI)
    xbc_r = xbc[:].rearrange("p (ch m) -> p ch m", m=515)
    MAT = 6 * 64

    def mats(ch, which):
        o = ch * MAT + {"acs": 0, "lm": 64, "expa": 128, "theta": 192, "alpha": 256}[which]
        return ssd_mats[:, o:o + 64]

    for g in range(NGRP):
        seq = g // 4
        tok_base = (g % 4) * TG
        first_in_seq = (g % 4 == 0)

        if g == 0:
            x_evts = x0_evts
        else:
            c.wait("gpsimd", "cdv", c.cnt["cdv"])
            x_evts = []
            for t in range(NTI):
                tok0 = tok_base + t * 128
                x_evts.append(c.dma("gpsimd", xreg[:, t * D:(t + 1) * D],
                                    x_in[seq, tok0:tok0 + 128, :], "qgp"))

        c.wait("vector", "cpe", c.cnt["cpe"])
        if prev_group_store[0] is not None:
            c.wait("vector", "qgp", prev_group_store[0][1])
        if first_in_seq:
            c.op("vector", lambda e: e.memset(s_bf[:], 0.0))
            carry_evt = c.op("vector", lambda e: e.memset(xbc_r[:, :, 0:3], 0.0), inc="cdv")
        else:
            carry_evt = c.op("vector", lambda e: e.tensor_copy(
                xbc_r[:, :, 0:3],
                carry_sb[:].rearrange("p (ch q) -> p ch q", q=3)), inc="cdv")

        for (tname, kind, tg) in DEBUG_TAPS:
            if tg == g and kind == "sbf":
                c.wait("gpsimd", "cdv", c.cnt["cdv"])
                c.dma("gpsimd", taps[tname], s_bf[:], "qgp")

        ln1_prev_tp = None
        for t in range(NTI):
            xt = xreg[:, t * D:(t + 1) * D]
            rstd_ev = ln_stats(xt, t, nm_scr[:].bitcast(BF16)[:, 0:D], src_evt=x_evts[t])
            c.wait_evt("vector", rstd_ev)
            c.wait_evt("vector", ln1_prev_tp)
            xn_ev = c.op("vector", lambda e, xt=xt, t=t: e.tensor_scalar(
                xn_tmp[:], xt, stat[:, 0 + t:1 + t], stat[:, 4 + t:5 + t],
                ALU.subtract, ALU.mult), inc="cdv")
            for kq in range(4):
                pi, wev = ring_next()
                c.wait_evt("tensor", wev)
                c.wait_evt("tensor", xn_ev)
                tp_ev = None
                for i in range(4):
                    k = kq * 4 + i
                    tp_ev = c.op("tensor", lambda e, pi=pi, i=i, k=k: e.transpose(
                        ps_gemm_bf[pi][:, i * 128:(i + 1) * 128],
                        xn_tmp[:, k * 128:(k + 1) * 128], ident_bf[:]),
                        inc=("cpe" if i == 3 else None))
                c.wait_evt("vector", tp_ev)
                ev = c.op("vector", lambda e, pi=pi, kq=kq, t=t: e.tensor_copy(
                    xn_fm[:].rearrange("p (k m) -> p k m", m=TG)[:, kq * 4:(kq + 1) * 4,
                                                                t * 128:(t + 1) * 128],
                    ps_gemm_bf[pi][:, 0:512].rearrange("p (k m) -> p k m", m=128)),
                    inc="cdv")
                ring_evac(pi, ev)
                ln1_prev_tp = tp_ev
        xn_ready = c.now("cdv")
        for (tname, kind, tg) in DEBUG_TAPS:
            if tg == g and kind == "xn_fm":
                c.wait("gpsimd", xn_ready[0], xn_ready[1])
                c.dma("gpsimd", taps[tname], xn_fm[:], "qgp")
            if tg == g and kind == "ln1dbg":
                c.wait("gpsimd", xn_ready[0], xn_ready[1])
                c.dma("gpsimd", taps[tname][:, 0:4 * D], xreg[:], "qgp")
                c.dma("gpsimd", taps[tname][:, 4 * D:4 * D + 24], stat[:], "qgp")

        b0, ld0 = load_strip(scr_ip[0][:, :], 4096, gate=(prep_ip if g == 0 else None))
        pi, wev = ring_next()
        c.wait_evt("tensor", wev)
        c.wait_evt("tensor", ld0)
        c.wait("tensor", xn_ready[0], xn_ready[1])
        dt_mm = None
        for k in range(NKD):
            dt_mm = c.op("tensor", lambda e, pi=pi, k=k, b0=b0: e.matmul(
                ps_gemm[pi][0:64, :], wsb[b0][:, k * 256:k * 256 + 64],
                xn_fm[:, k * TG:(k + 1) * TG], start=(k == 0), stop=(k == NKD - 1)),
                inc=("cpe" if k == NKD - 1 else None))
        strip["use"][b0] = dt_mm
        c.wait_evt("vector", dt_mm)
        c.wait_evt("vector", nega_evt)
        z_ev = c.op("vector", lambda e, pi=pi: e.tensor_scalar(
            dt_z[:], ps_gemm[pi][0:64, :], dtb[:], None, ALU.add), inc="cdv")
        ring_evac(pi, z_ev)
        c.wait_evt("scalar", z_ev)
        c.op("scalar", lambda e: e.activation(dt_t1[:], dt_z[:], AF.Abs))
        c.op("scalar", lambda e: e.activation(dt_t1[:], dt_t1[:], AF.Exp, scale=-1.0))
        ln1p_ev = c.op("scalar", lambda e: e.activation(dt_t1[:], dt_t1[:], AF.Ln, bias=misc_c[0:64, 1:2]),
                       inc="cac")
        c.op("vector", lambda e: e.tensor_scalar(dt_t2[:], dt_z[:], 0.0, None, ALU.max))
        c.wait_evt("vector", ln1p_ev)
        c.op("vector", lambda e: e.tensor_tensor(dt_z[:], dt_t2[:], dt_t1[:], ALU.add))
        lnin_ev = c.op("vector", lambda e: e.tensor_scalar(
            dt_t1[:], dt_z[:], 1e-30, None, ALU.max), inc="cdv")
        c.wait_evt("scalar", lnin_ev)
        lndt_ev = c.op("scalar", lambda e: e.activation(lndt_fm[:], dt_t1[:], AF.Ln), inc="cac")
        adt_ev = c.op("vector", lambda e: e.tensor_scalar(
            dt_t2[:], dt_z[:], nega[:], None, ALU.mult), inc="cdv")
        acs_evts = []
        for ch in range(NCH):
            acs_evts.append(c.op("vector", lambda e, ch=ch: e.tensor_tensor_scan(
                acs_fm[:, ch * 128:(ch + 1) * 128], dt_t2[:, ch * 128:(ch + 1) * 128],
                dt_t2[:, ch * 128:(ch + 1) * 128], 0.0, ALU.add, ALU.bypass), inc="cdv"))

        xbc_evt = [None] * NJX
        conv_evt = [None] * NJX
        cs_use = [None, None]
        cs_ctr = [0]
        xtm_evts = []

        def conv_chunk(oi):
            grp, pos = oi // 7, oi % 7
            if pos == 0:
                b = grp % 2
                c.wait_evt("sync", diag_st["use"][b])
                if g == 0:
                    c.wait("sync", prep_all[0], prep_all[1])
                else:
                    c.wait("sync", "cpe", c.cnt["cpe"])
                n = len(CONV_ORDER[grp * 7:(grp + 1) * 7])
                diag_st["ld"][b] = c.dma(
                    "sync", diag_sb[:, b * 7 * TG:b * 7 * TG + n * TG],
                    scr_conv.rearrange("cc p m -> p cc m")[:, grp * 7:grp * 7 + n, :], "qsy")
            db = grp % 2
            pi, wev = ring_next()
            c.wait_evt("tensor", wev)
            c.wait_evt("tensor", xbc_evt[oi])
            c.wait_evt("tensor", diag_st["ld"][db])
            cv_mm = None
            for k in range(CK):
                cv_mm = c.op("tensor", lambda e, pi=pi, k=k, db=db, pos=pos, oi=oi:
                             e.matmul(ps_gemm[pi][:],
                                      diag_sb[:, db * 7 * TG + pos * TG + k * 128:
                                              db * 7 * TG + pos * TG + (k + 1) * 128],
                                      xbc_r[:, oi, k:k + 512],
                                      start=(k == 0), stop=(k == CK - 1)),
                             inc=("cpe" if k == CK - 1 else None))
            if pos == 6 or oi == NJX - 1:
                diag_st["use"][db] = cv_mm
            cid = CONV_ORDER[oi]
            if cid == 32:
                dst = bc_fm[:, 0:TG]
            elif cid == 33:
                dst = bc_fm[:, TG:2 * TG]
            else:
                b2 = cs_ctr[0] % 2
                cs_ctr[0] += 1
                dst = convsilu[:, b2 * TG:(b2 + 1) * TG]
                c.wait_evt("scalar", cs_use[b2])
            c.wait_evt("scalar", cv_mm)
            ev = c.op("scalar", lambda e, pi=pi, dst=dst, cid=cid: e.activation(
                dst, ps_gemm[pi][:], AF.Silu, bias=convb[:, cid:cid + 1]), inc="cac")
            conv_evt[oi] = ev
            ring_evac(pi, ev)
            if cid < 32:
                f = cid
                pi2, wev2 = ring_next()
                c.wait_evt("tensor", wev2)
                c.wait_evt("tensor", ev)
                tp_ev = None
                for ch in range(NCH):
                    tp_ev = c.op("tensor", lambda e, pi2=pi2, ch=ch, dst=dst: e.transpose(
                        ps_gemm_bf[pi2][:, ch * 128:(ch + 1) * 128],
                        dst[:, ch * 128:(ch + 1) * 128], ident_bf[:]),
                        inc=("cpe" if ch == 3 else None))
                c.wait_evt("vector", tp_ev)
                ev2 = c.op("vector", lambda e, pi2=pi2, f=f: e.tensor_copy(
                    xtm_r[:, :, f * 128:(f + 1) * 128],
                    ps_gemm_bf[pi2][:, 0:512].rearrange("p (ch q) -> p ch q", q=128)),
                    inc="cdv")
                cs_use[(cs_ctr[0] - 1) % 2] = ev2
                ring_evac(pi2, ev2)
                xtm_evts.append(ev2)

        cur = [None]
        for oi in range(NJX):
            sb_i = 1 + oi // 2
            jl = oi % 2
            if jl == 0:
                cur[0] = load_strip(scr_ip[sb_i][:, :], 4096,
                                    gate=(prep_ip if g == 0 else None))
            bb, ldx = cur[0]
            pi, wev = ring_next()
            c.wait_evt("tensor", wev)
            c.wait_evt("tensor", ldx)
            mm_ev = None
            for k in range(NKD):
                mm_ev = c.op("tensor", lambda e, pi=pi, k=k, bb=bb, jl=jl: e.matmul(
                    ps_gemm[pi][:], wsb[bb][:, k * 256 + jl * 128:k * 256 + (jl + 1) * 128],
                    xn_fm[:, k * TG:(k + 1) * TG], start=(k == 0), stop=(k == NKD - 1)),
                    inc=("cpe" if k == NKD - 1 else None))
            if jl == 1 or oi == NJX - 1:
                strip["use"][bb] = mm_ev
            c.wait_evt("vector", mm_ev)
            c.wait_evt("vector", carry_evt)
            ev = c.op("vector", lambda e, pi=pi, oi=oi: e.tensor_copy(
                xbc_r[:, oi, 3:515], ps_gemm[pi][:]), inc="cdv")
            xbc_evt[oi] = ev
            ring_evac(pi, ev)
            conv_chunk(oi)

        c.op("vector", lambda e: e.tensor_copy(
            carry_sb[:].rearrange("p (ch q) -> p ch q", q=3), xbc_r[:, :, 512:515]),
            inc="cdv")

        gt_use = [None, None]
        sg_evts = []

        def emit_gate(i):
            sb_i = 18 + i // 2
            jl = i % 2
            if jl == 0:
                cur[0] = load_strip(scr_ip[sb_i][:, :], 4096,
                                    gate=(prep_all if g == 0 else None))
            bb, ldx = cur[0]
            pi, wev = ring_next()
            c.wait_evt("tensor", wev)
            c.wait_evt("tensor", ldx)
            mm_ev = None
            for k in range(NKD):
                mm_ev = c.op("tensor", lambda e, pi=pi, k=k, bb=bb, jl=jl: e.matmul(
                    ps_gemm[pi][:], wsb[bb][:, k * 256 + jl * 128:k * 256 + (jl + 1) * 128],
                    xn_fm[:, k * TG:(k + 1) * TG], start=(k == 0), stop=(k == NKD - 1)),
                    inc=("cpe" if k == NKD - 1 else None))
            if jl == 1:
                strip["use"][bb] = mm_ev
            b2 = i % 2
            c.wait_evt("vector", mm_ev)
            c.wait_evt("vector", gt_use[b2])
            gevac = c.op("vector", lambda e, pi=pi, b2=b2: e.tensor_copy(
                gtmp[:, b2 * TG:(b2 + 1) * TG], ps_gemm[pi][:]), inc="cdv")
            ring_evac(pi, gevac)
            pi2, wev2 = ring_next()
            c.wait_evt("tensor", wev2)
            c.wait_evt("tensor", gevac)
            tp_ev = None
            for t in range(NTI):
                tp_ev = c.op("tensor", lambda e, pi2=pi2, t=t, b2=b2: e.transpose(
                    ps_gemm_bf[pi2][:, t * 128:(t + 1) * 128],
                    gtmp[:, b2 * TG + t * 128:b2 * TG + (t + 1) * 128], ident_bf[:]),
                    inc=("cpe" if t == 3 else None))
            gt_use[b2] = tp_ev
            c.wait_evt("vector", tp_ev)
            sg_ev = c.op("vector", lambda e, pi2=pi2, i=i: e.tensor_copy(
                sg_tm[:].rearrange("p (t q) -> p t q", q=DI)[:, :, i * 128:(i + 1) * 128],
                ps_gemm_bf[pi2][:, 0:512].rearrange("p (t q) -> p t q", q=128)),
                inc="cdv")
            sg_evts.append(sg_ev)
            ring_evac(pi2, sg_ev)

        xtap_evt = None
        for (tname, kind, tg) in DEBUG_TAPS:
            if tg == g and kind == "x_tm":
                c.wait_evt("gpsimd", xtm_evts[-1])
                xtap_evt = c.dma("gpsimd", taps[tname], x_tm[:], "qgp")
            if tg == g and kind == "xbc":
                c.wait_evt("gpsimd", xtm_evts[-1])
                xtap_evt = c.dma("gpsimd", taps[tname], xbc[:], "qgp")

        prev_th = None
        prev_al = None
        for ch in range(NCH):
            c.wait_evt("tensor", acs_evts[ch])
            c.wait_evt("tensor", prev_th)
            c.wait_evt("tensor", prev_al)
            atp = c.op("tensor", lambda e, ch=ch: e.transpose(
                ps_m1t[0][:, 0:64], acs_fm[:, ch * 128:(ch + 1) * 128],
                ident_f[0:64, 0:64]), inc="cpe")
            c.wait_evt("vector", atp)
            c.wait_evt("vector", xtap_evt)
            acs_ev = c.op("vector", lambda e, ch=ch: e.tensor_copy(
                mats(ch, "acs"), ps_m1t[0][:, 0:64]), inc="cdv")
            c.wait_evt("tensor", lndt_ev)
            c.wait_evt("tensor", acs_ev)
            ltp = c.op("tensor", lambda e, ch=ch: e.transpose(
                ps_m1t[0][:, 64:128], lndt_fm[:, ch * 128:(ch + 1) * 128],
                ident_f[0:64, 0:64]), inc="cpe")
            c.wait_evt("vector", ltp)
            lm_ev = c.op("vector", lambda e, ch=ch: e.tensor_tensor(
                mats(ch, "lm"), ps_m1t[0][:, 64:128], mats(ch, "acs"), ALU.subtract),
                inc="cdv")
            c.wait_evt("scalar", acs_ev)
            expa_ev = c.op("scalar", lambda e, ch=ch: e.activation(
                mats(ch, "expa"), mats(ch, "acs"), AF.Exp), inc="cac")
            c.wait_evt("tensor", lm_ev)
            rtp = c.op("tensor", lambda e, ch=ch: e.transpose(
                ps_m1t[0][0:1, 128:192], acs_fm[:, ch * 128 + 127:ch * 128 + 128],
                ident_f[0:64, 0:64]), inc="cpe")
            c.wait_evt("vector", rtp)
            row_ev = c.op("vector", lambda e: e.tensor_copy(
                row_sb[:], ps_m1t[0][0:1, 128:192]), inc="cdv")
            c.wait_evt("tensor", row_ev)
            bc_mm = c.op("tensor", lambda e: e.matmul(
                ps_m1t[0][:, 192:256], ones_r[:], row_sb[:], start=True, stop=True),
                inc="cpe")
            c.wait_evt("vector", bc_mm)
            th_ev = c.op("vector", lambda e, ch=ch: e.tensor_tensor(
                mats(ch, "theta"), ps_m1t[0][:, 192:256], mats(ch, "lm"), ALU.add),
                inc="cdv")
            prev_th = th_ev
            c.wait_evt("scalar", th_ev)
            th2_ev = c.op("scalar", lambda e, ch=ch: e.activation(
                mats(ch, "theta"), mats(ch, "theta"), AF.Exp), inc="cac")
            al_ev = c.op("scalar", lambda e, ch=ch: e.activation(
                mats(ch, "alpha"), ps_m1t[0][:, 192:256], AF.Exp), inc="cac")
            prev_al = al_ev
            pi, wev = ring_next()
            c.wait_evt("tensor", wev)
            c.wait_evt("tensor", conv_evt[0])
            c.wait_evt("tensor", conv_evt[1])
            g_mm = c.op("tensor", lambda e, pi=pi, ch=ch: e.matmul(
                ps_gemm[pi][:, 0:128], bc_fm[:, ch * 128:(ch + 1) * 128],
                bc_fm[:, TG + ch * 128:TG + (ch + 1) * 128], start=True, stop=True),
                inc="cpe")
            btp = c.op("tensor", lambda e, pi=pi, ch=ch: e.transpose(
                ps_gemm_bf[pi][:, 256:384], bc_fm[:, ch * 128:(ch + 1) * 128], ident_bf[:]),
                inc="cpe")
            c.wait_evt("vector", g_mm)
            gm_ev = c.op("vector", lambda e, pi=pi, ch=ch: e.tensor_tensor(
                gcat[:, ch * 128:(ch + 1) * 128], ps_gemm[pi][:, 0:128], trimask[:],
                ALU.mult), inc="cdv")
            c.wait_evt("vector", btp)
            bt_ev = c.op("vector", lambda e, pi=pi, ch=ch: e.tensor_copy(
                btm_sb[:, ch * 128:(ch + 1) * 128], ps_gemm_bf[pi][:, 256:384]), inc="cdv")
            ring_evac(pi, bt_ev)
        prepass_dv = c.now("cdv")
        prepass_ac = c.now("cac")
        prepass_pe = c.now("cpe")

        last_tp = {}
        last_dmin = {}
        last_exp = {}
        last_pre = {}
        last_di = {}
        last_xp = {}
        last_t4 = {}
        last_yfin = {}
        last_smm = {}
        last_sstt = {}
        xtm_all = xtm_evts[-1]

        def m1t_slot(h, ch):
            s = (h % 2) * 4 + ch
            return m1t_sb[:, s * 128:(s + 1) * 128]

        def emit_pair(p):
            hp = (2 * p, 2 * p + 1)
            for h in hp:
                c.wait_evt("gpsimd", last_tp.get(h - 2))
                if p == 0:
                    c.wait("gpsimd", prepass_dv[0], prepass_dv[1])
                for ch in range(NCH):
                    c.op("gpsimd", lambda e, h=h, ch=ch: e.tensor_scalar(
                        m1t_slot(h, ch), ones_t[:], mats(ch, "acs")[:, h:h + 1], 0.0,
                        ALU.mult, ALU.add))
                c.wait_evt("gpsimd", last_yfin.get(h - 2))
                di_ev = c.op("gpsimd", lambda e, h=h: e.tensor_scalar(
                    di_sb[:, (h % 2) * 128:(h % 2 + 1) * 128], ident_bf[:],
                    dbc[:, h:h + 1], 0.0, ALU.mult, ALU.add), inc="cpo")
                last_di[h] = di_ev
            for h in hp:
                c.wait_evt("tensor", last_di[h])
                c.wait_evt("tensor", last_dmin.get(h - 2))
                if p == 0:
                    c.wait("tensor", prepass_ac[0], prepass_ac[1])
                    c.wait("tensor", prepass_dv[0], prepass_dv[1])
                tp_ev = None
                for ch in range(NCH):
                    tp_ev = c.op("tensor", lambda e, h=h, ch=ch: e.transpose(
                        ps_m1t[h % 2][:, ch * 128:(ch + 1) * 128], m1t_slot(h, ch),
                        ident_f[:]), inc=("cpe" if ch == 3 else None))
                last_tp[h] = tp_ev
            for h in hp:
                c.wait_evt("vector", last_tp[h])
                c.wait_evt("vector", last_exp.get(h - 2))
                dm_ev = None
                for ch in range(NCH):
                    dm_ev = c.op("vector", lambda e, h=h, ch=ch: e.tensor_scalar(
                        dmin4[:, (h % 2) * TG + ch * 128:(h % 2) * TG + (ch + 1) * 128],
                        ps_m1t[h % 2][:, ch * 128:(ch + 1) * 128],
                        mats(ch, "lm")[:, h:h + 1], EXP_CLAMP, ALU.add, ALU.min),
                        inc=("cdv" if ch == 3 else None))
                last_dmin[h] = dm_ev
            for h in hp:
                c.wait_evt("scalar", last_dmin[h])
                c.wait_evt("scalar", last_pre.get(h - 2))
                ex_ev = c.op("scalar", lambda e, h=h: e.activation(
                    l24[:, (h % 2) * TG:(h % 2 + 1) * TG],
                    dmin4[:, (h % 2) * TG:(h % 2 + 1) * TG], AF.Exp),
                    inc=("cac" if h % 2 == 1 else None))
            last_exp[hp[0]] = ex_ev
            last_exp[hp[1]] = ex_ev
            for h in hp:
                c.wait_evt("gpsimd", last_exp[h])
                if h >= 2:
                    c.wait_evt("gpsimd", last_smm.get((h - 2, NCH - 1)))
                pr_ev = c.op("gpsimd", lambda e, h=h: e.tensor_tensor(
                    lhy4[:, (h % 2) * TG:(h % 2 + 1) * TG],
                    l24[:, (h % 2) * TG:(h % 2 + 1) * TG], gcat[:], ALU.mult), inc="cpo")
                last_pre[h] = pr_ev
            for h in hp:
                c.wait_evt("vector", xtm_all)
                c.wait_evt("vector", last_smm.get((h - 2, NCH - 1)))
                xp_ev = None
                for ch in range(NCH):
                    xp_ev = c.op("vector", lambda e, h=h, ch=ch: e.tensor_scalar(
                        xp8[:, ((h % 2) * 4 + ch) * 64:((h % 2) * 4 + ch + 1) * 64],
                        xtm_r[:, ch, h * 64:(h + 1) * 64], mats(ch, "theta")[:, h:h + 1],
                        None, ALU.mult), inc=("cdv" if ch == 3 else None))
                last_xp[h] = xp_ev
            for ch in range(NCH):
                for h in hp:
                    if ch == 0:
                        c.wait_evt("tensor", last_pre[h])
                        c.wait_evt("tensor", last_xp[h])
                        c.wait_evt("tensor", last_yfin.get(h - 2))
                        c.wait_evt("tensor", last_t4.get(h - 2))
                        if p == 0:
                            c.wait("tensor", prepass_pe[0], prepass_pe[1])
                    else:
                        c.wait_evt("tensor", last_sstt[(h, ch - 1)])
                    yo = ps_yo[h % 2]
                    c.op("tensor", lambda e, h=h, ch=ch, yo=yo: e.matmul(
                        yo[:, ch * 64:(ch + 1) * 64],
                        lhy4[:, (h % 2) * TG + ch * 128:(h % 2) * TG + (ch + 1) * 128],
                        xtm_r[:, ch, h * 64:(h + 1) * 64], start=True, stop=False))
                    c.op("tensor", lambda e, h=h, ch=ch, yo=yo: e.matmul(
                        yo[:, ch * 64:(ch + 1) * 64],
                        di_sb[:, (h % 2) * 128:(h % 2 + 1) * 128],
                        xtm_r[:, ch, h * 64:(h + 1) * 64], start=False, stop=True))
                    c.op("tensor", lambda e, h=h, ch=ch, yo=yo: e.matmul(
                        yo[:, 256 + ch * 64:256 + (ch + 1) * 64],
                        bc_fm[:, TG + ch * 128:TG + (ch + 1) * 128],
                        s_bf[:, h * 64:(h + 1) * 64], start=True, stop=True))
                    smm = c.op("tensor", lambda e, h=h, ch=ch: e.matmul(
                        ps_s[h % 2][:, ch * 64:(ch + 1) * 64],
                        btm_sb[:, ch * 128:(ch + 1) * 128],
                        xp8[:, ((h % 2) * 4 + ch) * 64:((h % 2) * 4 + ch + 1) * 64],
                        start=True, stop=True), inc="cpe")
                    last_smm[(h, ch)] = smm
                for h in hp:
                    c.wait_evt("vector", last_smm[(h, ch)])
                    st_ev = c.op("vector", lambda e, h=h, ch=ch: e.scalar_tensor_tensor(
                        s_bf[:, h * 64:(h + 1) * 64], s_bf[:, h * 64:(h + 1) * 64],
                        mats(ch, "alpha")[:, h:h + 1],
                        ps_s[h % 2][:, ch * 64:(ch + 1) * 64], ALU.mult, ALU.add),
                        inc="cdv")
                    last_sstt[(h, ch)] = st_ev
            t4_ev = None
            for h in hp:
                c.wait_evt("scalar", last_smm[(h, NCH - 1)])
                c.wait_evt("scalar", last_yfin.get(h - 2))
                for ch in range(NCH):
                    t4_ev = c.op("scalar", lambda e, h=h, ch=ch: e.activation(
                        t4cat[:, (h % 2) * 256 + ch * 64:(h % 2) * 256 + (ch + 1) * 64],
                        ps_yo[h % 2][:, 256 + ch * 64:256 + (ch + 1) * 64], AF.Copy,
                        scale=mats(ch, "expa")[:, h:h + 1]),
                        inc=("cac" if (h % 2 == 1 and ch == 3) else None))
            last_t4[hp[0]] = t4_ev
            last_t4[hp[1]] = t4_ev
            for h in hp:
                c.wait_evt("vector", last_t4[h])
                yf_ev = c.op("vector", lambda e, h=h: e.tensor_tensor(
                    xtm_r[:, :, h * 64:(h + 1) * 64],
                    ps_yo[h % 2][:, 0:256].rearrange("p (ch q) -> p ch q", q=64),
                    t4cat[:, (h % 2) * 256:(h % 2) * 256 + 256].rearrange(
                        "p (ch q) -> p ch q", q=64), ALU.add), inc="cdv")
                last_yfin[h] = yf_ev

        for i in range(NJG):
            emit_gate(i)
            emit_pair(i)

        ytap_evt = None
        for (tname, kind, tg) in DEBUG_TAPS:
            if tg == g and kind == "y_tm":
                c.wait_evt("gpsimd", last_yfin[NH - 1])
                c.wait_evt("gpsimd", last_yfin[NH - 2])
                ytap_evt = c.dma("gpsimd", taps[tname], x_tm[:], "qgp")

        sgact_evts = []
        for t in range(NTI):
            st_ = sg_tm[:, t * DI:(t + 1) * DI]
            c.wait_evt("scalar", sg_evts[-1])
            sgact_evts.append(c.op("scalar", lambda e, st_=st_: e.activation(
                st_, st_, AF.Silu), inc="cac"))
        ynt_evts = []
        for t in range(NTI):
            yt = x_tm[:, t * DI:(t + 1) * DI]
            st_ = sg_tm[:, t * DI:(t + 1) * DI]
            c.wait_evt("vector", last_yfin[NH - 1])
            c.wait_evt("vector", last_yfin[NH - 2])
            c.wait_evt("vector", sgact_evts[t])
            c.wait_evt("vector", ytap_evt)
            yf_ev = c.op("vector", lambda e, yt=yt, st_=st_: e.tensor_tensor(
                yt, yt, st_, ALU.mult), inc="cdv")
            c.wait_evt("scalar", yf_ev)
            sq_ev = c.op("scalar", lambda e, yt=yt, t=t: e.activation(
                sqd[:], yt, AF.Square, accum_out=stat[:, 12 + t:13 + t]), inc="cac",
                drain=True)
            c.wait_evt("vector", sq_ev)
            v_ev = c.op("vector", lambda e, t=t: e.tensor_scalar(
                stat[:, 12 + t:13 + t], stat[:, 12 + t:13 + t], 1.0 / DI, None, ALU.mult),
                inc="cdv")
            c.wait_evt("scalar", v_ev)
            c.op("scalar", lambda e, t=t: e.activation(
                stat[:, 4 + t:5 + t], stat[:, 12 + t:13 + t], AF.Ln, bias=misc_c[:, 0:1]))
            c.drain("scalar")
            rs_ev = c.op("scalar", lambda e, t=t: e.activation(
                stat[:, 4 + t:5 + t], stat[:, 4 + t:5 + t], AF.Exp, scale=-0.5), inc="cac",
                drain=True)
            c.wait_evt("vector", rs_ev)
            yn_ev = c.op("vector", lambda e, yt=yt, t=t: e.tensor_scalar(
                yt, yt, stat[:, 4 + t:5 + t], None, ALU.mult), inc="cdv")
            for kq in range(8):
                pi, wev = ring_next()
                c.wait_evt("tensor", wev)
                c.wait_evt("tensor", yn_ev)
                tp_ev = None
                for i in range(4):
                    k = kq * 4 + i
                    tp_ev = c.op("tensor", lambda e, pi=pi, i=i, k=k, yt=yt: e.transpose(
                        ps_gemm_bf[pi][:, i * 128:(i + 1) * 128],
                        yt[:, k * 128:(k + 1) * 128], ident_bf[:]),
                        inc=("cpe" if i == 3 else None))
                c.wait_evt("vector", tp_ev)
                e_ev = None
                for i in range(4):
                    k = kq * 4 + i
                    e_ev = c.op("vector", lambda e, pi=pi, i=i, k=k, t=t: e.tensor_scalar(
                        yn_fm[:, k * TG + t * 128:k * TG + (t + 1) * 128],
                        ps_gemm_bf[pi][:, i * 128:(i + 1) * 128], normw[:, k:k + 1], None,
                        ALU.mult), inc=("cdv" if i == 3 else None))
                ring_evac(pi, e_ev)
                ynt_evts.append(e_ev)
        ynfm_ready = ynt_evts[-1]

        mo_evts = []
        for m in range(16):
            bb, ldx = load_strip(scr_m4[m][:, :], 4096, gate=(prep_all if g == 0 else None))
            pi, wev = ring_next()
            c.wait_evt("tensor", wev)
            c.wait_evt("tensor", ldx)
            c.wait_evt("tensor", ynfm_ready)
            mm_ev = None
            for k in range(NKI):
                mm_ev = c.op("tensor", lambda e, pi=pi, k=k, bb=bb: e.matmul(
                    ps_gemm[pi][:], wsb[bb][:, k * 128:(k + 1) * 128],
                    yn_fm[:, k * TG:(k + 1) * TG], start=(k == 0), stop=(k == NKI - 1)),
                    inc=("cpe" if k == NKI - 1 else None))
            strip["use"][bb] = mm_ev
            c.wait_evt("scalar", mm_ev)
            if m == 0:
                c.wait_evt("scalar", sg_evts[-1])
            mo_ev = c.op("scalar", lambda e, pi=pi, m=m: e.activation(
                mo_fm[:, m * TG:(m + 1) * TG], ps_gemm[pi][:], AF.Copy), inc="cac")
            mo_evts.append(mo_ev)
            ring_evac(pi, mo_ev)

        mt_evts = []
        for m in range(16):
            pi, wev = ring_next()
            c.wait_evt("tensor", wev)
            c.wait_evt("tensor", mo_evts[m])
            tp_ev = None
            for t in range(NTI):
                tp_ev = c.op("tensor", lambda e, pi=pi, t=t, m=m: e.transpose(
                    ps_gemm_bf[pi][:, t * 128:(t + 1) * 128],
                    mo_fm[:, m * TG + t * 128:m * TG + (t + 1) * 128], ident_bf[:]),
                    inc=("cpe" if t == 3 else None))
            c.wait_evt("vector", tp_ev)
            mt_ev = c.op("vector", lambda e, pi=pi, m=m: e.tensor_copy(
                mo_tm[:].rearrange("p (t q) -> p t q", q=D)[:, :, m * 128:(m + 1) * 128],
                ps_gemm_bf[pi][:, 0:512].rearrange("p (t q) -> p t q", q=128)), inc="cdv")
            mt_evts.append(mt_ev)
            ring_evac(pi, mt_ev)

        motap_evt = None
        for (tname, kind, tg) in DEBUG_TAPS:
            if tg == g and kind == "mo_tm":
                c.wait_evt("gpsimd", mt_evts[-1])
                motap_evt = c.dma("gpsimd", taps[tname], mo_tm[:], "qgp")

        x1_evts = []
        for t in range(NTI):
            c.wait_evt("vector", motap_evt)
            c.wait_evt("scalar", motap_evt)
            mt = mo_tm[:, t * D:(t + 1) * D]
            if t > 0:
                c.wait_evt("scalar", x1_evts[t - 1])
            rstd_ev = ln_stats(mt, t, nm_scr[:].bitcast(BF16)[:, 0:D], src_evt=mt_evts[-1])
            c.wait_evt("vector", rstd_ev)
            c.op("vector", lambda e, mt=mt, t=t: e.tensor_scalar(
                nm_scr[:], mt, stat[:, 0 + t:1 + t], stat[:, 4 + t:5 + t],
                ALU.subtract, ALU.mult))
            x1_ev = c.op("vector", lambda e, t=t: e.tensor_tensor(
                xreg[:, t * D:(t + 1) * D], nm_scr[:], xreg[:, t * D:(t + 1) * D],
                ALU.add), inc="cdv")
            x1_evts.append(x1_ev)

        ln3_prev_tp = None
        ln3_xn_evts = []
        for t in range(NTI):
            xt = xreg[:, t * D:(t + 1) * D]
            c.wait_evt("scalar", x1_evts[-1])
            rstd_ev = ln_stats(xt, t, nm_scr[:].bitcast(BF16)[:, 0:D], src_evt=x1_evts[t])
            c.wait_evt("vector", rstd_ev)
            c.wait_evt("vector", ln3_prev_tp)
            xn_ev = c.op("vector", lambda e, xt=xt, t=t: e.tensor_scalar(
                xn_tmp[:], xt, stat[:, 0 + t:1 + t], stat[:, 4 + t:5 + t],
                ALU.subtract, ALU.mult), inc="cdv")
            ln3_xn_evts.append(xn_ev)
            for kq in range(4):
                pi, wev = ring_next()
                c.wait_evt("tensor", wev)
                c.wait_evt("tensor", xn_ev)
                tp_ev = None
                for i in range(4):
                    k = kq * 4 + i
                    tp_ev = c.op("tensor", lambda e, pi=pi, i=i, k=k: e.transpose(
                        ps_gemm_bf[pi][:, i * 128:(i + 1) * 128],
                        xn_tmp[:, k * 128:(k + 1) * 128], ident_bf[:]),
                        inc=("cpe" if i == 3 else None))
                c.wait_evt("vector", tp_ev)
                ev = c.op("vector", lambda e, pi=pi, kq=kq, t=t: e.tensor_copy(
                    xn_fm[:].rearrange("p (k m) -> p k m", m=TG)[:, kq * 4:(kq + 1) * 4,
                                                                t * 128:(t + 1) * 128],
                    ps_gemm_bf[pi][:, 0:512].rearrange("p (k m) -> p k m", m=128)),
                    inc="cdv")
                ring_evac(pi, ev)
                ln3_prev_tp = tp_ev
        xn2_ready = c.now("cdv")

        hdn_last = None
        for j in range(NJF):
            bb, ldx = load_strip(scr_f1[j][:, :], 4096, gate=(prep_all if g == 0 else None))
            piA, wevA = ring_next()
            c.wait_evt("tensor", wevA)
            c.wait_evt("tensor", ldx)
            c.wait("tensor", xn2_ready[0], xn2_ready[1])
            gmm = None
            for k in range(NKD):
                gmm = c.op("tensor", lambda e, piA=piA, k=k, bb=bb: e.matmul(
                    ps_gemm[piA][:], wsb[bb][:, k * 128:(k + 1) * 128],
                    xn_fm[:, k * TG:(k + 1) * TG], start=(k == 0), stop=(k == NKD - 1)),
                    inc=("cpe" if k == NKD - 1 else None))
            piB, wevB = ring_next()
            c.wait_evt("tensor", wevB)
            umm = None
            for k in range(NKD):
                umm = c.op("tensor", lambda e, piB=piB, k=k, bb=bb: e.matmul(
                    ps_gemm[piB][:], wsb[bb][:, 2048 + k * 128:2048 + (k + 1) * 128],
                    xn_fm[:, k * TG:(k + 1) * TG], start=(k == 0), stop=(k == NKD - 1)),
                    inc=("cpe" if k == NKD - 1 else None))
            strip["use"][bb] = umm
            c.wait_evt("scalar", gmm)
            c.wait_evt("scalar", hdn_last)
            sil_ev = c.op("scalar", lambda e, piA=piA, j=j: e.activation(
                tsil[:], ps_gemm[piA][:], AF.Silu, scale=gs_c[:, j:j + 1]), inc="cac")
            ring_evac(piA, sil_ev)
            c.wait_evt("vector", sil_ev)
            c.wait_evt("vector", umm)
            hd_ev = c.op("vector", lambda e, piB=piB, j=j: e.scalar_tensor_tensor(
                hdn[:, j * TG:(j + 1) * TG], ps_gemm[piB][:], us_c[:, j:j + 1], tsil[:],
                ALU.mult, ALU.mult), inc="cdv")
            ring_evac(piB, hd_ev)
            hdn_last = hd_ev

        f2_evts = []
        for m in range(16):
            bb, ldx = load_strip(scr_f2[m][:, :], 5632, gate=(prep_all if g == 0 else None))
            pi, wev = ring_next()
            c.wait_evt("tensor", wev)
            c.wait_evt("tensor", ldx)
            c.wait_evt("tensor", hdn_last)
            mm_ev = None
            for j in range(NJF):
                mm_ev = c.op("tensor", lambda e, pi=pi, j=j, bb=bb: e.matmul(
                    ps_gemm[pi][:], wsb[bb][:, j * 128:(j + 1) * 128],
                    hdn[:, j * TG:(j + 1) * TG], start=(j == 0), stop=(j == NJF - 1)),
                    inc=("cpe" if j == NJF - 1 else None))
            strip["use"][bb] = mm_ev
            c.wait_evt("scalar", mm_ev)
            f2_ev = c.op("scalar", lambda e, pi=pi, m=m: e.activation(
                ffn_fm[:, m * TG:(m + 1) * TG], ps_gemm[pi][:], AF.Copy,
                scale=ds_c[:, m:m + 1]), inc="cac")
            f2_evts.append(f2_ev)
            ring_evac(pi, f2_ev)

        ft_evts = []
        for m in range(16):
            pi, wev = ring_next()
            c.wait_evt("tensor", wev)
            c.wait_evt("tensor", f2_evts[m])
            tp_ev = None
            for t in range(NTI):
                tp_ev = c.op("tensor", lambda e, pi=pi, t=t, m=m: e.transpose(
                    ps_gemm_bf[pi][:, t * 128:(t + 1) * 128],
                    ffn_fm[:, m * TG + t * 128:m * TG + (t + 1) * 128], ident_bf[:]),
                    inc=("cpe" if t == 3 else None))
            c.wait_evt("vector", tp_ev)
            ft_ev = c.op("vector", lambda e, pi=pi, m=m: e.tensor_copy(
                mo_tm[:].rearrange("p (t q) -> p t q", q=D)[:, :, m * 128:(m + 1) * 128],
                ps_gemm_bf[pi][:, 0:512].rearrange("p (t q) -> p t q", q=128)), inc="cdv")
            ft_evts.append(ft_ev)
            ring_evac(pi, ft_ev)

        st_evts = []
        x2_prev = None
        for t in range(NTI):
            mt = mo_tm[:, t * D:(t + 1) * D]
            c.wait_evt("scalar", x2_prev)
            rstd_ev = ln_stats(mt, t, nm_scr[:].bitcast(BF16)[:, 0:D], src_evt=ft_evts[-1])
            c.wait_evt("vector", rstd_ev)
            c.op("vector", lambda e, mt=mt, t=t: e.tensor_scalar(
                nm_scr[:], mt, stat[:, 0 + t:1 + t], stat[:, 4 + t:5 + t],
                ALU.subtract, ALU.mult))
            if t >= 2:
                c.wait("vector", "qgp", st_evts[t - 2][1])
            x2_ev = c.op("vector", lambda e, t=t: e.tensor_tensor(
                x2_scr[:, (t % 2) * D:(t % 2 + 1) * D], nm_scr[:],
                xreg[:, t * D:(t + 1) * D], ALU.add), inc="cdv", drain=True)
            x2_prev = x2_ev
            c.wait_evt("gpsimd", x2_ev)
            tok0 = tok_base + t * 128
            st_evts.append(c.dma("gpsimd", out[seq, tok0:tok0 + 128, :],
                                 x2_scr[:, (t % 2) * D:(t % 2 + 1) * D], "qgp"))
        prev_group_store[0] = st_evts[-1]

        for (tname, kind, tg) in DEBUG_TAPS:
            if tg != g:
                continue
            tap = taps[tname]
            if kind == "x1":
                c.wait_evt("gpsimd", x1_evts[-1])
                c.dma("gpsimd", tap, xreg[:], "qgp")
            elif kind == "yn_fm":
                c.wait_evt("gpsimd", ynt_evts[-1])
                c.dma("gpsimd", tap, yn_fm[:], "qgp")

            elif kind == "xn_fm":
                pass
            elif kind == "scr_ip":
                c.wait("gpsimd", prep_ip[0], prep_ip[1])
                for _s in range(34):
                    c.dma("gpsimd", tap[_s], scr_ip[_s], "qgp")
            elif kind == "acs":
                c.wait_evt("gpsimd", acs_evts[-1])
                c.wait_evt("gpsimd", lndt_ev)
                c.dma("gpsimd", tap[:, 0:TG], acs_fm[:], "qgp")
                c.dma("gpsimd", tap[:, TG:2 * TG], lndt_fm[:], "qgp")
                c.dma("gpsimd", tap[:, 2 * TG:3 * TG], dt_t2[:], "qgp")

    c.wait("gpsimd", "qgp", c.cnt["qgp"])

    with nc.Block() as block:
        def run(qn):
            def body(e):
                for fn in c.q[qn]:
                    fn(e)
            return body
        block.sync(run("sync"))
        block.tensor(run("tensor"))
        block.vector(run("vector"))
        block.scalar(run("scalar"))
        block.gpsimd(run("gpsimd"))

    for cm in c.sem_cm:
        cm.__exit__(None, None, None)
    return nc


def _mk_shared(ips):
    (w_ip, conv_w, conv_b, dt_bias, A_log, Dv, norm_w, w_op,
     gate_w, up_w, down_w, gate_s, up_s, down_s) = ips
    f32 = lambda a: np.ascontiguousarray(np.asarray(a, dtype=np.float32))

    def tile_t(wt, nk, nm, mw=128):
        return np.ascontiguousarray(
            wt.reshape(nk, 128, nm, mw).transpose(2, 1, 0, 3).reshape(nm, 128, nk * mw))

    wip_t = f32(w_ip).T
    wl_ip = np.zeros((34, 128, 4096), np.float32)
    wl_ip[18:34] = tile_t(np.ascontiguousarray(wip_t[:, 0:4096]), 16, 16, 256)
    wl_ip[2:18] = tile_t(np.ascontiguousarray(wip_t[:, 4096:8192]), 16, 16, 256)
    wl_ip[1] = tile_t(np.ascontiguousarray(wip_t[:, 8192:8448]), 16, 1, 256)[0]
    dt_part = np.ascontiguousarray(wip_t[:, 8448:8512]).reshape(16, 128, 64)
    for k in range(16):
        wl_ip[0, :, k * 256:k * 256 + 64] = dt_part[k]
    wl_m4 = tile_t(f32(w_op).T, 32, 16)
    f1g = tile_t(f32(gate_w).T, 16, 44)
    f1u = tile_t(f32(up_w).T, 16, 44)
    wl_f1 = np.ascontiguousarray(np.concatenate([f1g, f1u], axis=2))
    wl_f2 = tile_t(f32(down_w).T, 44, 16)
    cw = f32(conv_w)
    wl_cv = np.zeros((34, 128, 512), np.float32)
    pidx = np.arange(128)
    conv_order = [32, 33] + list(range(32))
    for oi, cid in enumerate(conv_order):
        for k in range(4):
            wl_cv[oi, pidx, k * 128 + pidx] = cw[cid * 128 + pidx, k]
    return {
        "wl_ip": wl_ip, "wl_m4": wl_m4, "wl_f1": wl_f1, "wl_f2": wl_f2, "wl_cv": wl_cv,
        "conv_bl": f32(np.asarray(conv_b, dtype=np.float32).reshape(34, 128).T),
        "a_log": f32(np.asarray(A_log, dtype=np.float32).reshape(64, 1)),
        "dt_bias": f32(np.asarray(dt_bias, dtype=np.float32).reshape(64, 1)),
        "d_bc": f32(np.broadcast_to(np.asarray(Dv, dtype=np.float32).reshape(1, 64),
                                    (128, 64))),
        "norm_wl": f32(np.asarray(norm_w, dtype=np.float32).reshape(32, 128).T),
        "gs_l": f32(np.asarray(gate_s, dtype=np.float32).reshape(44, 128).T),
        "us_l": f32(np.asarray(up_s, dtype=np.float32).reshape(44, 128).T),
        "ds_l": f32(np.asarray(down_s, dtype=np.float32).reshape(16, 128).T),
        "ident_bf": _bf(np.eye(128)),
        "ident_f": np.eye(128, dtype=np.float32),
        "trimask": np.triu(np.ones((128, 128), dtype=np.float32)),
        "ones_t": np.ones((128, 128), dtype=np.float32),
        "ones_r": np.ones((1, 128), dtype=np.float32),
        "misc_l": np.ascontiguousarray(
            np.broadcast_to(np.array([1e-5, 1.0], dtype=np.float32), (128, 2))),
    }


def kernel(L0_attn_hidden_states, L0_attn_in_proj_w, L0_attn_conv_w, L0_attn_conv_b,
           L0_attn_dt_bias, L0_attn_A_log, L0_attn_D, L0_attn_norm_w, L0_attn_out_proj_w,
           L0_ffn_gate_w, L0_ffn_up_w, L0_ffn_down_w, L0_ffn_gate_s, L0_ffn_up_s,
           L0_ffn_down_s):
    x = np.asarray(L0_attn_hidden_states, dtype=np.float32)
    shared = _mk_shared((L0_attn_in_proj_w, L0_attn_conv_w, L0_attn_conv_b,
                         L0_attn_dt_bias, L0_attn_A_log, L0_attn_D, L0_attn_norm_w,
                         L0_attn_out_proj_w, L0_ffn_gate_w, L0_ffn_up_w, L0_ffn_down_w,
                         L0_ffn_gate_s, L0_ffn_up_s, L0_ffn_down_s))
    nc = build_program()
    in_maps = []
    for i in range(NCORE):
        m = dict(shared)
        m["x"] = np.ascontiguousarray(x[2 * i:2 * i + 2])
        in_maps.append(m)
    res = run_bass_kernel_spmd(nc, in_maps, core_ids=list(range(NCORE)))
    return np.concatenate([r["o"] for r in res.results], axis=0)


# version 15
# speedup vs baseline: 1.1873x; 1.1873x over previous
import numpy as np
import ml_dtypes
import concourse.bass as bass
import concourse.mybir as mybir
from concourse.bass_utils import run_bass_kernel_spmd

F32 = mybir.dt.float32
BF16 = mybir.dt.bfloat16
AF = mybir.ActivationFunctionType
ALU = mybir.AluOpType
AXX = mybir.AxisListType.X

D = 2048
DI = 4096
NH = 64
HD = 64
CK = 4
PROJ = 8512
INTER = 5632
TG = 512
NTI = 4
NCH = 4
NGRP = 8
SEQLEN = 2048
NCORE = 8
NKD = 16
NKI = 32
NJG = 32
NJX = 34
NJF = 44
EXP_CLAMP = 80.0

CONV_ORDER = [32, 33] + list(range(32))

DEBUG_TAPS = []


def _bf(a):
    return np.asarray(a).astype(ml_dtypes.bfloat16)


ENGS = ("sync", "tensor", "vector", "scalar", "gpsimd")


class Ctx:
    def __init__(self, nc):
        self.nc = nc
        self.q = {e: [] for e in ENGS}
        self.sem_cm = []
        self.sems = {}
        self.cnt = {}
        self.pe_log = []
        self.cur_phase = "init"

    def phase(self, name):
        self.cur_phase = name

    def sem(self, name):
        cm = self.nc.semaphore(name)
        self.sems[name] = cm.__enter__()
        self.sem_cm.append(cm)
        self.cnt[name] = 0

    def emit(self, eng, fn):
        self.q[eng].append(fn)

    def wait(self, eng, sname, val):
        if val <= 0:
            return
        if eng == "tensor":
            self.pe_log.append(("WAIT", self.cur_phase))
        h = self.sems[sname]
        self.emit(eng, lambda e, h=h, v=val: e.wait_ge(h, v))

    def wait_evt(self, eng, evt):
        if evt is not None:
            self.wait(eng, evt[0], evt[1])

    def dma(self, eng, out, in_, sname):
        self.cnt[sname] += 16
        tgt = self.cnt[sname]
        h = self.sems[sname]
        self.emit(eng, lambda e, o=out, i=in_, h=h: e.dma_start(out=o, in_=i).then_inc(h, 16))
        return (sname, tgt)

    def dma_tp(self, eng, out, in_, sname):
        self.cnt[sname] += 16
        tgt = self.cnt[sname]
        h = self.sems[sname]
        self.emit(eng, lambda e, o=out, i=in_, h=h: e.dma_start_transpose(o, i).then_inc(h, 16))
        return (sname, tgt)

    def op(self, eng, builder, inc=None, drain=None):
        if eng == "tensor":
            self.pe_log.append(("OP", self.cur_phase))
        if inc is None:
            self.emit(eng, builder)
            return None
        self.cnt[inc] += 1
        tgt = self.cnt[inc]
        h = self.sems[inc]
        if drain is None:
            drain = eng in ("scalar", "gpsimd")
        if not drain:
            self.emit(eng, lambda e, b=builder, h=h: b(e).then_inc(h, 1))
        else:
            if eng == "tensor":
                self.pe_log.append(("DRAIN", self.cur_phase))
            self.emit(eng, lambda e, b=builder, h=h: (b(e), e.drain().then_inc(h, 1)))
        return (inc, tgt)

    def drain(self, eng):
        if eng == "tensor":
            self.pe_log.append(("DRAIN", self.cur_phase))
        self.emit(eng, lambda e: e.drain())

    def now(self, sname):
        return (sname, self.cnt[sname])


def build_program():
    nc = bass.Bass("TRN2", target_bir_lowering=False)
    c = Ctx(nc)

    def din(name, shape, dt=F32):
        return nc.dram_tensor(name, list(shape), dt, kind="ExternalInput").ap()

    x_in = din("x", (2, SEQLEN, D))
    wl_ip = din("wl_ip", (34, 128, 4096))
    wl_m4 = din("wl_m4", (16, 128, 4096))
    wl_f1 = din("wl_f1", (44, 128, 4096))
    wl_f2 = din("wl_f2", (16, 128, 5632))
    wl_cv = din("wl_cv", (34, 128, 512))
    conv_bl = din("conv_bl", (128, 34))
    a_log = din("a_log", (64, 1))
    dt_bias_i = din("dt_bias", (64, 1))
    d_bc = din("d_bc", (128, 64))
    norm_wl = din("norm_wl", (128, 32))
    gs_l = din("gs_l", (128, 44))
    us_l = din("us_l", (128, 44))
    ds_l = din("ds_l", (128, 16))
    ident_bf_i = din("ident_bf", (128, 128), BF16)
    ident_f_i = din("ident_f", (128, 128), F32)
    trimask_i = din("trimask", (128, 128), F32)
    ones_t_i = din("ones_t", (128, 128), F32)
    ones_r_i = din("ones_r", (1, 128), F32)
    misc_i = din("misc_l", (128, 2), F32)

    out = nc.dram_tensor("o", [2, SEQLEN, D], F32, kind="ExternalOutput").ap()

    def scr(name, shape, dt=BF16):
        return nc.dram_tensor(name, list(shape), dt, kind="Internal").ap()

    scr_ip = scr("scr_ip", (34, 128, 4096))
    scr_conv = scr("scr_conv", (34, 128, 512))
    scr_m4 = scr("scr_m4", (16, 128, 4096))
    scr_f1 = scr("scr_f1", (44, 128, 4096))
    scr_f2 = scr("scr_f2", (16, 128, 5632))

    taps = {}
    for (tname, kind, tg) in DEBUG_TAPS:
        shp = {"x_tm": [128, NCH * DI], "y_tm": [128, NCH * DI], "x1": [128, NTI * D],
               "xbc": [128, NJX * 515], "acs": [64, 3 * TG],
               "xn_fm": [128, NKD * TG], "scr_ip": [34, 128, 4096],
               "ln1dbg": [128, 4 * D + 64], "sbf": [128, NH * HD]}[kind]
        taps[tname] = nc.dram_tensor(tname, shp, F32, kind="ExternalOutput").ap()

    def sb_at(name, shape, dt, off):
        return nc.alloc_sbuf_tensor_at(name, list(shape), dt, offset=off + 16512)

    ident_bf = sb_at("ident_bf_s", [128, 128], BF16, 0)
    ident_f = sb_at("ident_f_s", [128, 128], F32, 256)
    trimask = sb_at("trimask_s", [128, 128], F32, 768)
    ones_t = sb_at("ones_t_s", [128, 128], F32, 1280)
    ones_r = sb_at("ones_r_s", [1, 128], F32, 1792)
    convb = sb_at("convb_s", [128, 34], F32, 2848)
    nega = sb_at("nega_s", [64, 1], F32, 3008)
    dtb = sb_at("dtb_s", [64, 1], F32, 3040)
    dbc = sb_at("dbc_s", [128, 64], F32, 3072)
    normw = sb_at("normw_s", [128, 32], F32, 3328)
    gs_c = sb_at("gs_s", [128, 44], F32, 3456)
    us_c = sb_at("us_s", [128, 44], F32, 3648)
    ds_c = sb_at("ds_s", [128, 16], F32, 3840)
    stat = sb_at("stat_s", [128, 24], F32, 211968)
    row_sb = sb_at("row_s", [1, 64], F32, 212064)
    di_sb = sb_at("di_s", [128, 256], BF16, 212320)
    misc_c = sb_at("misc_s", [128, 2], F32, 3904)
    xreg = sb_at("xreg", [128, NTI * D], BF16, 4096)
    xn_tmp = sb_at("xn_tmp", [128, D], BF16, 20480)
    xn_fm = sb_at("xn_fm", [128, NKD * TG], BF16, 24576)
    mo_tm = sb_at("mo_tm", [128, NTI * D], BF16, 20480)
    gtmp = sb_at("gtmp", [128, 2 * TG], BF16, 40960)
    sg_tm = sb_at("sg_tm", [128, NTI * DI], BF16, 43008)
    mo_fm = sb_at("mo_fm", [128, 16 * TG], BF16, 43008)
    ffn_fm = sb_at("ffn_fm", [128, 16 * TG], BF16, 43008)
    tsil = sb_at("tsil", [128, TG], F32, 59392)
    xbc = sb_at("xbc", [128, NJX * 515], BF16, 75776)
    ssd_mats = sb_at("ssd_mats", [128, NCH * 6 * 64], F32, 75776)
    dmin4 = sb_at("dmin4", [128, 2 * TG], F32, 81920)
    l24 = sb_at("l24", [128, 2 * TG], BF16, 86016)
    lhy4 = sb_at("lhy4", [128, 2 * TG], BF16, 88064)
    gcat = sb_at("gcat", [128, TG], BF16, 90112)
    t4cat = sb_at("t4cat", [128, 512], F32, 91136)
    xp8 = sb_at("xp8", [128, 512], BF16, 93184)
    m1t_sb = sb_at("m1t_sb", [128, 8 * 128], F32, 94208)
    btm_sb = sb_at("btm_sb", [128, NCH * 128], BF16, 98304)
    yn_fm = sb_at("yn_fm", [128, NKI * TG], BF16, 75776)
    x2_scr = sb_at("x2_scr", [128, 2 * D], F32, 75776)
    x_tm = sb_at("x_tm", [128, NCH * DI], BF16, 111616)
    hdn = sb_at("hdn", [128, NJF * TG], BF16, 111616)
    convsilu = sb_at("convsilu", [128, 2 * TG], BF16, 144384)
    bc_fm = sb_at("bc_fm", [128, 2 * TG], BF16, 146432)
    diag_sb = sb_at("diag_sb", [128, 2 * 7 * TG], BF16, 148480)
    sqd = sb_at("sqd", [128, DI], BF16, 148480)
    nm_scr = sb_at("nm_scr", [128, D], F32, 162816)
    s_bf = sb_at("s_bf", [128, NH * HD], BF16, 171008)
    dt_z = sb_at("dt_z", [64, TG], F32, 179200)
    dt_t1 = sb_at("dt_t1", [64, TG], F32, 181248)
    dt_t2 = sb_at("dt_t2", [64, TG], F32, 183296)
    acs_fm = sb_at("acs_fm", [64, TG], F32, 185344)
    carry_sb = sb_at("carry_s", [128, 34 * 3], BF16, 181248)
    lndt_fm = sb_at("lndt_fm", [64, TG], F32, 187392)
    wsb = [sb_at(f"wsb{i}", [128, 5632], BF16, 189440 + i * 11264) for i in range(2)]

    ps_gemm = [nc.alloc_psum_tensor(f"ps_gemm{i}", [128, 512], F32) for i in range(2)]
    ps_m1t = [nc.alloc_psum_tensor(f"ps_m1t{i}", [128, 512], F32) for i in range(2)]
    ps_yo = [nc.alloc_psum_tensor(f"ps_yo{i}", [128, 512], F32) for i in range(2)]
    ps_s = [nc.alloc_psum_tensor(f"ps_s{i}", [128, 512], F32) for i in range(2)]
    ps_gemm_bf = [p[:].bitcast(BF16) for p in ps_gemm]

    for s in ("qsy", "qgp", "qsc", "cpe", "cdv", "cac", "cpo", "qip", "qcv"):
        c.sem(s)

    x0_evts = []
    for t in range(NTI):
        x0_evts.append(c.dma("gpsimd", xreg[:, t * D:(t + 1) * D],
                             x_in[0, t * 128:(t + 1) * 128, :], "qgp"))

    const_last = None
    for dst, srcap in ((ident_bf, ident_bf_i), (ident_f, ident_f_i), (trimask, trimask_i),
                       (ones_t, ones_t_i), (ones_r, ones_r_i),
                       (convb, conv_bl), (dbc, d_bc), (normw, norm_wl), (gs_c, gs_l),
                       (us_c, us_l), (ds_c, ds_l), (nega, a_log), (dtb, dt_bias_i),
                       (misc_c, misc_i)):
        const_last = c.dma("sync", dst[:], srcap, "qsy")
    c.wait_evt("scalar", const_last)
    c.op("scalar", lambda e: e.activation(nega[:], nega[:], AF.Exp))
    nega_evt = c.op("scalar", lambda e: e.mul(nega[:], nega[:], -1.0), inc="cac")

    def cast_lay(srcap, dst, nslab, per):
        ev = None
        evs = []
        for s0 in range(0, nslab, per):
            s1 = min(s0 + per, nslab)
            ev = c.dma("gpsimd", dst[s0:s1], srcap[s0:s1], "qgp")
            evs.extend([ev] * (s1 - s0))
        return ev, evs

    prep_ip, _ = cast_lay(wl_ip, scr_ip, 34, 3)
    prep_conv_done, _ = cast_lay(wl_cv, scr_conv, 34, 17)
    prep_m4, _ = cast_lay(wl_m4, scr_m4, 16, 3)
    prep_f1, _ = cast_lay(wl_f1, scr_f1, 44, 3)
    prep_f2, _ = cast_lay(wl_f2, scr_f2, 16, 2)
    prep_all = c.now("qgp")

    ring = {"idx": 0, "evac": [None, None]}

    def ring_next():
        i = ring["idx"] % 2
        ev = ring["evac"][i]
        ring["idx"] += 1
        return i, ev

    def ring_evac(i, evt):
        ring["evac"][i] = evt

    strip = {"ctr": 0, "use": [None, None]}

    def load_strip(src_ap, width, gate=None):
        b = strip["ctr"] % 2
        strip["ctr"] += 1
        c.wait_evt("sync", strip["use"][b])
        if gate is not None:
            c.wait("sync", gate[0], gate[1])
        ld = c.dma("sync", wsb[b][:, 0:width], src_ap, "qsy")
        return b, ld

    diag_st = {"ctr": 0, "use": [None, None], "ld": [None, None]}
    prev_group_store = [None]

    ln_state = {"redsq": None}

    def ln_stats(src, t, sq_dump, src_evt=None):
        width = src.shape[-1]
        c.wait_evt("vector", src_evt)
        c.wait_evt("scalar", src_evt)
        c.op("vector", lambda e: e.reduce_sum(stat[:, 8 + t:9 + t], src, axis=AXX))
        sq_ev = c.op("scalar", lambda e: e.activation(
            sq_dump, src, AF.Square, accum_out=stat[:, 12 + t:13 + t]), inc="cac")
        c.wait_evt("vector", sq_ev)
        c.drain("vector")
        c.op("vector", lambda e: e.tensor_scalar(
            stat[:, 0 + t:1 + t], stat[:, 8 + t:9 + t], 1.0 / width, None, ALU.mult))
        c.op("vector", lambda e: e.tensor_scalar(
            stat[:, 12 + t:13 + t], stat[:, 12 + t:13 + t], 1.0 / width, None, ALU.mult))
        c.drain("vector")
        c.op("vector", lambda e: e.tensor_tensor(
            stat[:, 16 + t:17 + t], stat[:, 0 + t:1 + t], stat[:, 0 + t:1 + t], ALU.mult))
        c.drain("vector")
        var_ev = c.op("vector", lambda e: e.tensor_tensor(
            stat[:, 12 + t:13 + t], stat[:, 12 + t:13 + t], stat[:, 16 + t:17 + t],
            ALU.subtract), inc="cdv")
        c.wait_evt("scalar", var_ev)
        c.op("scalar", lambda e: e.activation(
            stat[:, 4 + t:5 + t], stat[:, 12 + t:13 + t], AF.Ln, bias=misc_c[:, 0:1]))
        c.drain("scalar")
        rstd_ev = c.op("scalar", lambda e: e.activation(
            stat[:, 4 + t:5 + t], stat[:, 4 + t:5 + t], AF.Exp, scale=-0.5), inc="cac")
        return rstd_ev

    xtm_r = x_tm[:].rearrange("p (ch q) -> p ch q", q=DI)
    xbc_r = xbc[:].rearrange("p (ch m) -> p ch m", m=515)
    MAT = 6 * 64

    def mats(ch, which):
        o = ch * MAT + {"acs": 0, "lm": 64, "expa": 128, "theta": 192, "alpha": 256}[which]
        return ssd_mats[:, o:o + 64]

    for g in range(NGRP):
        seq = g // 4
        tok_base = (g % 4) * TG
        first_in_seq = (g % 4 == 0)

        c.phase(f"g{g}:xload")
        if g == 0:
            x_evts = x0_evts
        else:
            c.wait("gpsimd", "cdv", c.cnt["cdv"])
            x_evts = []
            for t in range(NTI):
                tok0 = tok_base + t * 128
                x_evts.append(c.dma("gpsimd", xreg[:, t * D:(t + 1) * D],
                                    x_in[seq, tok0:tok0 + 128, :], "qgp"))

        c.phase(f"g{g}:carry")
        c.wait("vector", "cpe", c.cnt["cpe"])
        if prev_group_store[0] is not None:
            c.wait("vector", "qgp", prev_group_store[0][1])
        if first_in_seq:
            c.op("vector", lambda e: e.memset(s_bf[:], 0.0))
            carry_evt = c.op("vector", lambda e: e.memset(xbc_r[:, :, 0:3], 0.0), inc="cdv")
        else:
            carry_evt = c.op("vector", lambda e: e.tensor_copy(
                xbc_r[:, :, 0:3],
                carry_sb[:].rearrange("p (ch q) -> p ch q", q=3)), inc="cdv")

        for (tname, kind, tg) in DEBUG_TAPS:
            if tg == g and kind == "sbf":
                c.wait("gpsimd", "cdv", c.cnt["cdv"])
                c.dma("gpsimd", taps[tname], s_bf[:], "qgp")

        c.phase(f"g{g}:ln1")
        ln1_prev_tp = None
        for t in range(NTI):
            xt = xreg[:, t * D:(t + 1) * D]
            rstd_ev = ln_stats(xt, t, nm_scr[:].bitcast(BF16)[:, 0:D], src_evt=x_evts[t])
            c.wait_evt("vector", rstd_ev)
            c.wait_evt("vector", ln1_prev_tp)
            xn_ev = c.op("vector", lambda e, xt=xt, t=t: e.tensor_scalar(
                xn_tmp[:], xt, stat[:, 0 + t:1 + t], stat[:, 4 + t:5 + t],
                ALU.subtract, ALU.mult), inc="cdv")
            for kq in range(4):
                pi, wev = ring_next()
                c.wait_evt("tensor", wev)
                c.wait_evt("tensor", xn_ev)
                tp_ev = None
                for i in range(4):
                    k = kq * 4 + i
                    tp_ev = c.op("tensor", lambda e, pi=pi, i=i, k=k: e.transpose(
                        ps_gemm_bf[pi][:, i * 128:(i + 1) * 128],
                        xn_tmp[:, k * 128:(k + 1) * 128], ident_bf[:]),
                        inc=("cpe" if i == 3 else None))
                c.wait_evt("vector", tp_ev)
                ev = c.op("vector", lambda e, pi=pi, kq=kq, t=t: e.tensor_copy(
                    xn_fm[:].rearrange("p (k m) -> p k m", m=TG)[:, kq * 4:(kq + 1) * 4,
                                                                t * 128:(t + 1) * 128],
                    ps_gemm_bf[pi][:, 0:512].rearrange("p (k m) -> p k m", m=128)),
                    inc="cdv")
                ring_evac(pi, ev)
                ln1_prev_tp = tp_ev
        xn_ready = c.now("cdv")
        for (tname, kind, tg) in DEBUG_TAPS:
            if tg == g and kind == "xn_fm":
                c.wait("gpsimd", xn_ready[0], xn_ready[1])
                c.dma("gpsimd", taps[tname], xn_fm[:], "qgp")
            if tg == g and kind == "ln1dbg":
                c.wait("gpsimd", xn_ready[0], xn_ready[1])
                c.dma("gpsimd", taps[tname][:, 0:4 * D], xreg[:], "qgp")
                c.dma("gpsimd", taps[tname][:, 4 * D:4 * D + 24], stat[:], "qgp")

        c.phase(f"g{g}:m1dt")
        b0, ld0 = load_strip(scr_ip[0][:, :], 4096, gate=(prep_ip if g == 0 else None))
        pi, wev = ring_next()
        c.wait_evt("tensor", wev)
        c.wait_evt("tensor", ld0)
        c.wait("tensor", xn_ready[0], xn_ready[1])
        dt_mm = None
        for k in range(NKD):
            dt_mm = c.op("tensor", lambda e, pi=pi, k=k, b0=b0: e.matmul(
                ps_gemm[pi][0:64, :], wsb[b0][:, k * 256:k * 256 + 64],
                xn_fm[:, k * TG:(k + 1) * TG], start=(k == 0), stop=(k == NKD - 1)),
                inc=("cpe" if k == NKD - 1 else None))
        strip["use"][b0] = dt_mm
        c.wait_evt("vector", dt_mm)
        c.wait_evt("vector", nega_evt)
        z_ev = c.op("vector", lambda e, pi=pi: e.tensor_scalar(
            dt_z[:], ps_gemm[pi][0:64, :], dtb[:], None, ALU.add), inc="cdv")
        ring_evac(pi, z_ev)
        c.wait_evt("scalar", z_ev)
        c.op("scalar", lambda e: e.activation(dt_t1[:], dt_z[:], AF.Abs))
        c.op("scalar", lambda e: e.activation(dt_t1[:], dt_t1[:], AF.Exp, scale=-1.0))
        ln1p_ev = c.op("scalar", lambda e: e.activation(dt_t1[:], dt_t1[:], AF.Ln, bias=misc_c[0:64, 1:2]),
                       inc="cac")
        c.op("vector", lambda e: e.tensor_scalar(dt_t2[:], dt_z[:], 0.0, None, ALU.max))
        c.wait_evt("vector", ln1p_ev)
        c.op("vector", lambda e: e.tensor_tensor(dt_z[:], dt_t2[:], dt_t1[:], ALU.add))
        lnin_ev = c.op("vector", lambda e: e.tensor_scalar(
            dt_t1[:], dt_z[:], 1e-30, None, ALU.max), inc="cdv")
        c.wait_evt("scalar", lnin_ev)
        lndt_ev = c.op("scalar", lambda e: e.activation(lndt_fm[:], dt_t1[:], AF.Ln), inc="cac")
        adt_ev = c.op("vector", lambda e: e.tensor_scalar(
            dt_t2[:], dt_z[:], nega[:], None, ALU.mult), inc="cdv")
        acs_evts = []
        for ch in range(NCH):
            acs_evts.append(c.op("vector", lambda e, ch=ch: e.tensor_tensor_scan(
                acs_fm[:, ch * 128:(ch + 1) * 128], dt_t2[:, ch * 128:(ch + 1) * 128],
                dt_t2[:, ch * 128:(ch + 1) * 128], 0.0, ALU.add, ALU.bypass), inc="cdv"))

        c.phase(f"g{g}:m1xbc")
        xbc_evt = [None] * NJX
        conv_evt = [None] * NJX
        cs_use = [None, None]
        cs_ctr = [0]
        xtm_evts = []

        def conv_chunk(oi):
            grp, pos = oi // 7, oi % 7
            if pos == 0:
                b = grp % 2
                c.wait_evt("sync", diag_st["use"][b])
                if g == 0:
                    c.wait("sync", prep_all[0], prep_all[1])
                else:
                    c.wait("sync", "cpe", c.cnt["cpe"])
                n = len(CONV_ORDER[grp * 7:(grp + 1) * 7])
                diag_st["ld"][b] = c.dma(
                    "sync", diag_sb[:, b * 7 * TG:b * 7 * TG + n * TG],
                    scr_conv.rearrange("cc p m -> p cc m")[:, grp * 7:grp * 7 + n, :], "qsy")
            db = grp % 2
            pi, wev = ring_next()
            c.wait_evt("tensor", wev)
            c.wait_evt("tensor", xbc_evt[oi])
            c.wait_evt("tensor", diag_st["ld"][db])
            cv_mm = None
            for k in range(CK):
                cv_mm = c.op("tensor", lambda e, pi=pi, k=k, db=db, pos=pos, oi=oi:
                             e.matmul(ps_gemm[pi][:],
                                      diag_sb[:, db * 7 * TG + pos * TG + k * 128:
                                              db * 7 * TG + pos * TG + (k + 1) * 128],
                                      xbc_r[:, oi, k:k + 512],
                                      start=(k == 0), stop=(k == CK - 1)),
                             inc=("cpe" if k == CK - 1 else None))
            if pos == 6 or oi == NJX - 1:
                diag_st["use"][db] = cv_mm
            cid = CONV_ORDER[oi]
            if cid == 32:
                dst = bc_fm[:, 0:TG]
            elif cid == 33:
                dst = bc_fm[:, TG:2 * TG]
            else:
                b2 = cs_ctr[0] % 2
                cs_ctr[0] += 1
                dst = convsilu[:, b2 * TG:(b2 + 1) * TG]
                c.wait_evt("scalar", cs_use[b2])
            c.wait_evt("scalar", cv_mm)
            ev = c.op("scalar", lambda e, pi=pi, dst=dst, cid=cid: e.activation(
                dst, ps_gemm[pi][:], AF.Silu, bias=convb[:, cid:cid + 1]), inc="cac")
            conv_evt[oi] = ev
            ring_evac(pi, ev)
            return (dst, cid, ev, (cs_ctr[0] - 1) % 2)

        def conv_tp(state):
            dst, cid, ev, b2u = state
            if cid >= 32:
                return
            f = cid
            pi2, wev2 = ring_next()
            c.wait_evt("tensor", wev2)
            c.wait_evt("tensor", ev)
            tp_ev = None
            for ch in range(NCH):
                tp_ev = c.op("tensor", lambda e, pi2=pi2, ch=ch, dst=dst: e.transpose(
                    ps_gemm_bf[pi2][:, ch * 128:(ch + 1) * 128],
                    dst[:, ch * 128:(ch + 1) * 128], ident_bf[:]),
                    inc=("cpe" if ch == 3 else None))
            c.wait_evt("vector", tp_ev)
            ev2 = c.op("vector", lambda e, pi2=pi2, f=f: e.tensor_copy(
                xtm_r[:, :, f * 128:(f + 1) * 128],
                ps_gemm_bf[pi2][:, 0:512].rearrange("p (ch q) -> p ch q", q=128)),
                inc="cdv")
            cs_use[b2u] = ev2
            ring_evac(pi2, ev2)
            xtm_evts.append(ev2)

        cur = [None]
        pend = [None]
        for oi in range(NJX):
            sb_i = 1 + oi // 2
            jl = oi % 2
            if jl == 0:
                cur[0] = load_strip(scr_ip[sb_i][:, :], 4096,
                                    gate=(prep_ip if g == 0 else None))
            bb, ldx = cur[0]
            pi, wev = ring_next()
            c.wait_evt("tensor", wev)
            c.wait_evt("tensor", ldx)
            mm_ev = None
            for k in range(NKD):
                mm_ev = c.op("tensor", lambda e, pi=pi, k=k, bb=bb, jl=jl: e.matmul(
                    ps_gemm[pi][:], wsb[bb][:, k * 256 + jl * 128:k * 256 + (jl + 1) * 128],
                    xn_fm[:, k * TG:(k + 1) * TG], start=(k == 0), stop=(k == NKD - 1)),
                    inc=("cpe" if k == NKD - 1 else None))
            if jl == 1 or oi == NJX - 1:
                strip["use"][bb] = mm_ev
            c.wait_evt("vector", mm_ev)
            c.wait_evt("vector", carry_evt)
            ev = c.op("vector", lambda e, pi=pi, oi=oi: e.tensor_copy(
                xbc_r[:, oi, 3:515], ps_gemm[pi][:]), inc="cdv")
            xbc_evt[oi] = ev
            ring_evac(pi, ev)
            if oi > 0:
                st = conv_chunk(oi - 1)
                if pend[0] is not None:
                    conv_tp(pend[0])
                pend[0] = st
        st = conv_chunk(NJX - 1)
        conv_tp(pend[0])
        conv_tp(st)

        c.op("vector", lambda e: e.tensor_copy(
            carry_sb[:].rearrange("p (ch q) -> p ch q", q=3), xbc_r[:, :, 512:515]),
            inc="cdv")

        gt_use = [None, None]
        sg_evts = []

        def emit_gate(i):
            c.phase(f"g{g}:gate")
            sb_i = 18 + i // 2
            jl = i % 2
            if jl == 0:
                cur[0] = load_strip(scr_ip[sb_i][:, :], 4096,
                                    gate=(prep_ip if g == 0 else None))
            bb, ldx = cur[0]
            pi, wev = ring_next()
            c.wait_evt("tensor", wev)
            c.wait_evt("tensor", ldx)
            mm_ev = None
            for k in range(NKD):
                mm_ev = c.op("tensor", lambda e, pi=pi, k=k, bb=bb, jl=jl: e.matmul(
                    ps_gemm[pi][:], wsb[bb][:, k * 256 + jl * 128:k * 256 + (jl + 1) * 128],
                    xn_fm[:, k * TG:(k + 1) * TG], start=(k == 0), stop=(k == NKD - 1)),
                    inc=("cpe" if k == NKD - 1 else None))
            if jl == 1:
                strip["use"][bb] = mm_ev
            b2 = i % 2
            c.wait_evt("vector", mm_ev)
            c.wait_evt("vector", gt_use[b2])
            gevac = c.op("vector", lambda e, pi=pi, b2=b2: e.tensor_copy(
                gtmp[:, b2 * TG:(b2 + 1) * TG], ps_gemm[pi][:]), inc="cdv")
            ring_evac(pi, gevac)
            pi2, wev2 = ring_next()
            c.wait_evt("tensor", wev2)
            c.wait_evt("tensor", gevac)
            tp_ev = None
            for t in range(NTI):
                tp_ev = c.op("tensor", lambda e, pi2=pi2, t=t, b2=b2: e.transpose(
                    ps_gemm_bf[pi2][:, t * 128:(t + 1) * 128],
                    gtmp[:, b2 * TG + t * 128:b2 * TG + (t + 1) * 128], ident_bf[:]),
                    inc=("cpe" if t == 3 else None))
            gt_use[b2] = tp_ev
            c.wait_evt("vector", tp_ev)
            sg_ev = c.op("vector", lambda e, pi2=pi2, i=i: e.tensor_copy(
                sg_tm[:].rearrange("p (t q) -> p t q", q=DI)[:, :, i * 128:(i + 1) * 128],
                ps_gemm_bf[pi2][:, 0:512].rearrange("p (t q) -> p t q", q=128)),
                inc="cdv")
            sg_evts.append(sg_ev)
            ring_evac(pi2, sg_ev)

        xtap_evt = None
        for (tname, kind, tg) in DEBUG_TAPS:
            if tg == g and kind == "x_tm":
                c.wait_evt("gpsimd", xtm_evts[-1])
                xtap_evt = c.dma("gpsimd", taps[tname], x_tm[:], "qgp")
            if tg == g and kind == "xbc":
                c.wait_evt("gpsimd", xtm_evts[-1])
                xtap_evt = c.dma("gpsimd", taps[tname], xbc[:], "qgp")

        c.phase(f"g{g}:prepass")
        prev_th = None
        prev_al = None
        for ch in range(NCH):
            c.wait_evt("tensor", acs_evts[ch])
            c.wait_evt("tensor", prev_th)
            c.wait_evt("tensor", prev_al)
            atp = c.op("tensor", lambda e, ch=ch: e.transpose(
                ps_m1t[0][:, 0:64], acs_fm[:, ch * 128:(ch + 1) * 128],
                ident_f[0:64, 0:64]), inc="cpe")
            c.wait_evt("vector", atp)
            c.wait_evt("vector", xtap_evt)
            acs_ev = c.op("vector", lambda e, ch=ch: e.tensor_copy(
                mats(ch, "acs"), ps_m1t[0][:, 0:64]), inc="cdv")
            c.wait_evt("tensor", lndt_ev)
            c.wait_evt("tensor", acs_ev)
            ltp = c.op("tensor", lambda e, ch=ch: e.transpose(
                ps_m1t[0][:, 64:128], lndt_fm[:, ch * 128:(ch + 1) * 128],
                ident_f[0:64, 0:64]), inc="cpe")
            c.wait_evt("vector", ltp)
            lm_ev = c.op("vector", lambda e, ch=ch: e.tensor_tensor(
                mats(ch, "lm"), ps_m1t[0][:, 64:128], mats(ch, "acs"), ALU.subtract),
                inc="cdv")
            c.wait_evt("scalar", acs_ev)
            expa_ev = c.op("scalar", lambda e, ch=ch: e.activation(
                mats(ch, "expa"), mats(ch, "acs"), AF.Exp), inc="cac")
            c.wait_evt("tensor", lm_ev)
            rtp = c.op("tensor", lambda e, ch=ch: e.transpose(
                ps_m1t[0][0:1, 128:192], acs_fm[:, ch * 128 + 127:ch * 128 + 128],
                ident_f[0:64, 0:64]), inc="cpe")
            c.wait_evt("vector", rtp)
            row_ev = c.op("vector", lambda e: e.tensor_copy(
                row_sb[:], ps_m1t[0][0:1, 128:192]), inc="cdv")
            c.wait_evt("tensor", row_ev)
            bc_mm = c.op("tensor", lambda e: e.matmul(
                ps_m1t[0][:, 192:256], ones_r[:], row_sb[:], start=True, stop=True),
                inc="cpe")
            c.wait_evt("vector", bc_mm)
            th_ev = c.op("vector", lambda e, ch=ch: e.tensor_tensor(
                mats(ch, "theta"), ps_m1t[0][:, 192:256], mats(ch, "lm"), ALU.add),
                inc="cdv")
            prev_th = th_ev
            c.wait_evt("scalar", th_ev)
            th2_ev = c.op("scalar", lambda e, ch=ch: e.activation(
                mats(ch, "theta"), mats(ch, "theta"), AF.Exp), inc="cac")
            al_ev = c.op("scalar", lambda e, ch=ch: e.activation(
                mats(ch, "alpha"), ps_m1t[0][:, 192:256], AF.Exp), inc="cac")
            prev_al = al_ev
            pi, wev = ring_next()
            c.wait_evt("tensor", wev)
            c.wait_evt("tensor", conv_evt[0])
            c.wait_evt("tensor", conv_evt[1])
            g_mm = c.op("tensor", lambda e, pi=pi, ch=ch: e.matmul(
                ps_gemm[pi][:, 0:128], bc_fm[:, ch * 128:(ch + 1) * 128],
                bc_fm[:, TG + ch * 128:TG + (ch + 1) * 128], start=True, stop=True),
                inc="cpe")
            btp = c.op("tensor", lambda e, pi=pi, ch=ch: e.transpose(
                ps_gemm_bf[pi][:, 256:384], bc_fm[:, ch * 128:(ch + 1) * 128], ident_bf[:]),
                inc="cpe")
            c.wait_evt("vector", g_mm)
            gm_ev = c.op("vector", lambda e, pi=pi, ch=ch: e.tensor_tensor(
                gcat[:, ch * 128:(ch + 1) * 128], ps_gemm[pi][:, 0:128], trimask[:],
                ALU.mult), inc="cdv")
            c.wait_evt("vector", btp)
            bt_ev = c.op("vector", lambda e, pi=pi, ch=ch: e.tensor_copy(
                btm_sb[:, ch * 128:(ch + 1) * 128], ps_gemm_bf[pi][:, 256:384]), inc="cdv")
            ring_evac(pi, bt_ev)
        prepass_dv = c.now("cdv")
        prepass_ac = c.now("cac")
        prepass_pe = c.now("cpe")

        last_tp = {}
        last_dmin = {}
        last_exp = {}
        last_pre = {}
        last_di = {}
        last_xp = {}
        last_t4 = {}
        last_yfin = {}
        last_smm = {}
        last_sstt = {}
        xtm_all = xtm_evts[-1]

        def m1t_slot(h, ch):
            s = (h % 2) * 4 + ch
            return m1t_sb[:, s * 128:(s + 1) * 128]

        def emit_pair(p):
            c.phase(f"g{g}:ssdpair")
            hp = (2 * p, 2 * p + 1)
            for h in hp:
                c.wait_evt("gpsimd", last_tp.get(h - 2))
                if p == 0:
                    c.wait("gpsimd", prepass_dv[0], prepass_dv[1])
                for ch in range(NCH):
                    c.op("gpsimd", lambda e, h=h, ch=ch: e.tensor_scalar(
                        m1t_slot(h, ch), ones_t[:], mats(ch, "acs")[:, h:h + 1], 0.0,
                        ALU.mult, ALU.add))
                c.wait_evt("gpsimd", last_yfin.get(h - 2))
                di_ev = c.op("gpsimd", lambda e, h=h: e.tensor_scalar(
                    di_sb[:, (h % 2) * 128:(h % 2 + 1) * 128], ident_bf[:],
                    dbc[:, h:h + 1], 0.0, ALU.mult, ALU.add), inc="cpo")
                last_di[h] = di_ev
            for h in hp:
                c.wait_evt("tensor", last_di[h])
                c.wait_evt("tensor", last_dmin.get(h - 2))
                if p == 0:
                    c.wait("tensor", prepass_ac[0], prepass_ac[1])
                    c.wait("tensor", prepass_dv[0], prepass_dv[1])
                tp_ev = None
                for ch in range(NCH):
                    tp_ev = c.op("tensor", lambda e, h=h, ch=ch: e.transpose(
                        ps_m1t[h % 2][:, ch * 128:(ch + 1) * 128], m1t_slot(h, ch),
                        ident_f[:]), inc=("cpe" if ch == 3 else None))
                last_tp[h] = tp_ev
            for h in hp:
                c.wait_evt("vector", last_tp[h])
                c.wait_evt("vector", last_exp.get(h - 2))
                dm_ev = None
                for ch in range(NCH):
                    dm_ev = c.op("vector", lambda e, h=h, ch=ch: e.tensor_scalar(
                        dmin4[:, (h % 2) * TG + ch * 128:(h % 2) * TG + (ch + 1) * 128],
                        ps_m1t[h % 2][:, ch * 128:(ch + 1) * 128],
                        mats(ch, "lm")[:, h:h + 1], EXP_CLAMP, ALU.add, ALU.min),
                        inc=("cdv" if ch == 3 else None))
                last_dmin[h] = dm_ev
            for h in hp:
                c.wait_evt("scalar", last_dmin[h])
                c.wait_evt("scalar", last_pre.get(h - 2))
                ex_ev = c.op("scalar", lambda e, h=h: e.activation(
                    l24[:, (h % 2) * TG:(h % 2 + 1) * TG],
                    dmin4[:, (h % 2) * TG:(h % 2 + 1) * TG], AF.Exp),
                    inc=("cac" if h % 2 == 1 else None))
            last_exp[hp[0]] = ex_ev
            last_exp[hp[1]] = ex_ev
            for h in hp:
                c.wait_evt("vector", last_exp[h])
                if h >= 2:
                    c.wait_evt("vector", last_smm.get((h - 2, NCH - 1)))
                pr_ev = c.op("vector", lambda e, h=h: e.tensor_tensor(
                    lhy4[:, (h % 2) * TG:(h % 2 + 1) * TG],
                    l24[:, (h % 2) * TG:(h % 2 + 1) * TG], gcat[:], ALU.mult), inc="cdv")
                last_pre[h] = pr_ev
            for h in hp:
                c.wait_evt("vector", xtm_all)
                c.wait_evt("vector", last_smm.get((h - 2, NCH - 1)))
                xp_ev = None
                for ch in range(NCH):
                    xp_ev = c.op("vector", lambda e, h=h, ch=ch: e.tensor_scalar(
                        xp8[:, ((h % 2) * 4 + ch) * 64:((h % 2) * 4 + ch + 1) * 64],
                        xtm_r[:, ch, h * 64:(h + 1) * 64], mats(ch, "theta")[:, h:h + 1],
                        None, ALU.mult), inc=("cdv" if ch == 3 else None))
                last_xp[h] = xp_ev
            for ch in range(NCH):
                for h in hp:
                    if ch == 0:
                        c.wait_evt("tensor", last_pre[h])
                        c.wait_evt("tensor", last_xp[h])
                        c.wait_evt("tensor", last_yfin.get(h - 2))
                        c.wait_evt("tensor", last_t4.get(h - 2))
                        if p == 0:
                            c.wait("tensor", prepass_pe[0], prepass_pe[1])
                    else:
                        c.wait_evt("tensor", last_sstt[(h, ch - 1)])
                    yo = ps_yo[h % 2]
                    c.op("tensor", lambda e, h=h, ch=ch, yo=yo: e.matmul(
                        yo[:, ch * 64:(ch + 1) * 64],
                        lhy4[:, (h % 2) * TG + ch * 128:(h % 2) * TG + (ch + 1) * 128],
                        xtm_r[:, ch, h * 64:(h + 1) * 64], start=True, stop=False))
                    c.op("tensor", lambda e, h=h, ch=ch, yo=yo: e.matmul(
                        yo[:, ch * 64:(ch + 1) * 64],
                        di_sb[:, (h % 2) * 128:(h % 2 + 1) * 128],
                        xtm_r[:, ch, h * 64:(h + 1) * 64], start=False, stop=True))
                    c.op("tensor", lambda e, h=h, ch=ch, yo=yo: e.matmul(
                        yo[:, 256 + ch * 64:256 + (ch + 1) * 64],
                        bc_fm[:, TG + ch * 128:TG + (ch + 1) * 128],
                        s_bf[:, h * 64:(h + 1) * 64], start=True, stop=True))
                    smm = c.op("tensor", lambda e, h=h, ch=ch: e.matmul(
                        ps_s[h % 2][:, ch * 64:(ch + 1) * 64],
                        btm_sb[:, ch * 128:(ch + 1) * 128],
                        xp8[:, ((h % 2) * 4 + ch) * 64:((h % 2) * 4 + ch + 1) * 64],
                        start=True, stop=True), inc="cpe")
                    last_smm[(h, ch)] = smm
                for h in hp:
                    c.wait_evt("vector", last_smm[(h, ch)])
                    st_ev = c.op("vector", lambda e, h=h, ch=ch: e.scalar_tensor_tensor(
                        s_bf[:, h * 64:(h + 1) * 64], s_bf[:, h * 64:(h + 1) * 64],
                        mats(ch, "alpha")[:, h:h + 1],
                        ps_s[h % 2][:, ch * 64:(ch + 1) * 64], ALU.mult, ALU.add),
                        inc="cdv")
                    last_sstt[(h, ch)] = st_ev
            t4_ev = None
            for h in hp:
                c.wait_evt("scalar", last_smm[(h, NCH - 1)])
                c.wait_evt("scalar", last_yfin.get(h - 2))
                for ch in range(NCH):
                    t4_ev = c.op("scalar", lambda e, h=h, ch=ch: e.activation(
                        t4cat[:, (h % 2) * 256 + ch * 64:(h % 2) * 256 + (ch + 1) * 64],
                        ps_yo[h % 2][:, 256 + ch * 64:256 + (ch + 1) * 64], AF.Copy,
                        scale=mats(ch, "expa")[:, h:h + 1]),
                        inc=("cac" if (h % 2 == 1 and ch == 3) else None))
            last_t4[hp[0]] = t4_ev
            last_t4[hp[1]] = t4_ev
            for h in hp:
                c.wait_evt("vector", last_t4[h])
                yf_ev = c.op("vector", lambda e, h=h: e.tensor_tensor(
                    xtm_r[:, :, h * 64:(h + 1) * 64],
                    ps_yo[h % 2][:, 0:256].rearrange("p (ch q) -> p ch q", q=64),
                    t4cat[:, (h % 2) * 256:(h % 2) * 256 + 256].rearrange(
                        "p (ch q) -> p ch q", q=64), ALU.add), inc="cdv")
                last_yfin[h] = yf_ev

        for i in range(NJG):
            emit_gate(i)
            emit_pair(i)

        ytap_evt = None
        for (tname, kind, tg) in DEBUG_TAPS:
            if tg == g and kind == "y_tm":
                c.wait_evt("gpsimd", last_yfin[NH - 1])
                c.wait_evt("gpsimd", last_yfin[NH - 2])
                ytap_evt = c.dma("gpsimd", taps[tname], x_tm[:], "qgp")

        c.phase(f"g{g}:gnorm_m4")
        sgact_evts = []
        for t in range(NTI):
            st_ = sg_tm[:, t * DI:(t + 1) * DI]
            c.wait_evt("scalar", sg_evts[-1])
            sgact_evts.append(c.op("scalar", lambda e, st_=st_: e.activation(
                st_, st_, AF.Silu), inc="cac"))
        ynt_evts = []
        for t in range(NTI):
            yt = x_tm[:, t * DI:(t + 1) * DI]
            st_ = sg_tm[:, t * DI:(t + 1) * DI]
            c.wait_evt("vector", last_yfin[NH - 1])
            c.wait_evt("vector", last_yfin[NH - 2])
            c.wait_evt("vector", sgact_evts[t])
            c.wait_evt("vector", ytap_evt)
            yf_ev = c.op("vector", lambda e, yt=yt, st_=st_: e.tensor_tensor(
                yt, yt, st_, ALU.mult), inc="cdv")
            c.wait_evt("scalar", yf_ev)
            sq_ev = c.op("scalar", lambda e, yt=yt, t=t: e.activation(
                sqd[:], yt, AF.Square, accum_out=stat[:, 12 + t:13 + t]), inc="cac")
            c.wait_evt("vector", sq_ev)
            v_ev = c.op("vector", lambda e, t=t: e.tensor_scalar(
                stat[:, 12 + t:13 + t], stat[:, 12 + t:13 + t], 1.0 / DI, None, ALU.mult),
                inc="cdv")
            c.wait_evt("scalar", v_ev)
            c.op("scalar", lambda e, t=t: e.activation(
                stat[:, 4 + t:5 + t], stat[:, 12 + t:13 + t], AF.Ln, bias=misc_c[:, 0:1]))
            c.drain("scalar")
            rs_ev = c.op("scalar", lambda e, t=t: e.activation(
                stat[:, 4 + t:5 + t], stat[:, 4 + t:5 + t], AF.Exp, scale=-0.5), inc="cac")
            c.wait_evt("vector", rs_ev)
            yn_ev = c.op("vector", lambda e, yt=yt, t=t: e.tensor_scalar(
                yt, yt, stat[:, 4 + t:5 + t], None, ALU.mult), inc="cdv")
            for kq in range(8):
                pi, wev = ring_next()
                c.wait_evt("tensor", wev)
                c.wait_evt("tensor", yn_ev)
                tp_ev = None
                for i in range(4):
                    k = kq * 4 + i
                    tp_ev = c.op("tensor", lambda e, pi=pi, i=i, k=k, yt=yt: e.transpose(
                        ps_gemm_bf[pi][:, i * 128:(i + 1) * 128],
                        yt[:, k * 128:(k + 1) * 128], ident_bf[:]),
                        inc=("cpe" if i == 3 else None))
                c.wait_evt("vector", tp_ev)
                e_ev = None
                for i in range(4):
                    k = kq * 4 + i
                    e_ev = c.op("vector", lambda e, pi=pi, i=i, k=k, t=t: e.tensor_scalar(
                        yn_fm[:, k * TG + t * 128:k * TG + (t + 1) * 128],
                        ps_gemm_bf[pi][:, i * 128:(i + 1) * 128], normw[:, k:k + 1], None,
                        ALU.mult), inc=("cdv" if i == 3 else None))
                ring_evac(pi, e_ev)
                ynt_evts.append(e_ev)
        ynfm_ready = ynt_evts[-1]

        c.phase(f"g{g}:m4")
        mo_evts = []
        for m in range(16):
            bb, ldx = load_strip(scr_m4[m][:, :], 4096, gate=(prep_all if g == 0 else None))
            pi, wev = ring_next()
            c.wait_evt("tensor", wev)
            c.wait_evt("tensor", ldx)
            c.wait_evt("tensor", ynfm_ready)
            mm_ev = None
            for k in range(NKI):
                mm_ev = c.op("tensor", lambda e, pi=pi, k=k, bb=bb: e.matmul(
                    ps_gemm[pi][:], wsb[bb][:, k * 128:(k + 1) * 128],
                    yn_fm[:, k * TG:(k + 1) * TG], start=(k == 0), stop=(k == NKI - 1)),
                    inc=("cpe" if k == NKI - 1 else None))
            strip["use"][bb] = mm_ev
            c.wait_evt("scalar", mm_ev)
            if m == 0:
                c.wait_evt("scalar", sg_evts[-1])
            mo_ev = c.op("scalar", lambda e, pi=pi, m=m: e.activation(
                mo_fm[:, m * TG:(m + 1) * TG], ps_gemm[pi][:], AF.Copy), inc="cac")
            mo_evts.append(mo_ev)
            ring_evac(pi, mo_ev)

        mt_evts = []
        for m in range(16):
            pi, wev = ring_next()
            c.wait_evt("tensor", wev)
            c.wait_evt("tensor", mo_evts[m])
            tp_ev = None
            for t in range(NTI):
                tp_ev = c.op("tensor", lambda e, pi=pi, t=t, m=m: e.transpose(
                    ps_gemm_bf[pi][:, t * 128:(t + 1) * 128],
                    mo_fm[:, m * TG + t * 128:m * TG + (t + 1) * 128], ident_bf[:]),
                    inc=("cpe" if t == 3 else None))
            c.wait_evt("vector", tp_ev)
            mt_ev = c.op("vector", lambda e, pi=pi, m=m: e.tensor_copy(
                mo_tm[:].rearrange("p (t q) -> p t q", q=D)[:, :, m * 128:(m + 1) * 128],
                ps_gemm_bf[pi][:, 0:512].rearrange("p (t q) -> p t q", q=128)), inc="cdv")
            mt_evts.append(mt_ev)
            ring_evac(pi, mt_ev)

        x1_evts = []
        for t in range(NTI):
            mt = mo_tm[:, t * D:(t + 1) * D]
            if t > 0:
                c.wait_evt("scalar", x1_evts[t - 1])
            rstd_ev = ln_stats(mt, t, nm_scr[:].bitcast(BF16)[:, 0:D], src_evt=mt_evts[-1])
            c.wait_evt("vector", rstd_ev)
            c.op("vector", lambda e, mt=mt, t=t: e.tensor_scalar(
                nm_scr[:], mt, stat[:, 0 + t:1 + t], stat[:, 4 + t:5 + t],
                ALU.subtract, ALU.mult))
            x1_ev = c.op("vector", lambda e, t=t: e.tensor_tensor(
                xreg[:, t * D:(t + 1) * D], nm_scr[:], xreg[:, t * D:(t + 1) * D],
                ALU.add), inc="cdv")
            x1_evts.append(x1_ev)

        c.phase(f"g{g}:ln3")
        ln3_prev_tp = None
        ln3_xn_evts = []
        for t in range(NTI):
            xt = xreg[:, t * D:(t + 1) * D]
            c.wait_evt("scalar", x1_evts[-1])
            rstd_ev = ln_stats(xt, t, nm_scr[:].bitcast(BF16)[:, 0:D], src_evt=x1_evts[t])
            c.wait_evt("vector", rstd_ev)
            c.wait_evt("vector", ln3_prev_tp)
            xn_ev = c.op("vector", lambda e, xt=xt, t=t: e.tensor_scalar(
                xn_tmp[:], xt, stat[:, 0 + t:1 + t], stat[:, 4 + t:5 + t],
                ALU.subtract, ALU.mult), inc="cdv")
            ln3_xn_evts.append(xn_ev)
            for kq in range(4):
                pi, wev = ring_next()
                c.wait_evt("tensor", wev)
                c.wait_evt("tensor", xn_ev)
                tp_ev = None
                for i in range(4):
                    k = kq * 4 + i
                    tp_ev = c.op("tensor", lambda e, pi=pi, i=i, k=k: e.transpose(
                        ps_gemm_bf[pi][:, i * 128:(i + 1) * 128],
                        xn_tmp[:, k * 128:(k + 1) * 128], ident_bf[:]),
                        inc=("cpe" if i == 3 else None))
                c.wait_evt("vector", tp_ev)
                ev = c.op("vector", lambda e, pi=pi, kq=kq, t=t: e.tensor_copy(
                    xn_fm[:].rearrange("p (k m) -> p k m", m=TG)[:, kq * 4:(kq + 1) * 4,
                                                                t * 128:(t + 1) * 128],
                    ps_gemm_bf[pi][:, 0:512].rearrange("p (k m) -> p k m", m=128)),
                    inc="cdv")
                ring_evac(pi, ev)
                ln3_prev_tp = tp_ev
        xn2_ready = c.now("cdv")

        c.phase(f"g{g}:f1")
        hdn_last = None
        for j in range(NJF):
            bb, ldx = load_strip(scr_f1[j][:, :], 4096, gate=(prep_all if g == 0 else None))
            piA, wevA = ring_next()
            c.wait_evt("tensor", wevA)
            c.wait_evt("tensor", ldx)
            c.wait("tensor", xn2_ready[0], xn2_ready[1])
            gmm = None
            for k in range(NKD):
                gmm = c.op("tensor", lambda e, piA=piA, k=k, bb=bb: e.matmul(
                    ps_gemm[piA][:], wsb[bb][:, k * 128:(k + 1) * 128],
                    xn_fm[:, k * TG:(k + 1) * TG], start=(k == 0), stop=(k == NKD - 1)),
                    inc=("cpe" if k == NKD - 1 else None))
            piB, wevB = ring_next()
            c.wait_evt("tensor", wevB)
            umm = None
            for k in range(NKD):
                umm = c.op("tensor", lambda e, piB=piB, k=k, bb=bb: e.matmul(
                    ps_gemm[piB][:], wsb[bb][:, 2048 + k * 128:2048 + (k + 1) * 128],
                    xn_fm[:, k * TG:(k + 1) * TG], start=(k == 0), stop=(k == NKD - 1)),
                    inc=("cpe" if k == NKD - 1 else None))
            strip["use"][bb] = umm
            c.wait_evt("scalar", gmm)
            c.wait_evt("scalar", hdn_last)
            sil_ev = c.op("scalar", lambda e, piA=piA, j=j: e.activation(
                tsil[:], ps_gemm[piA][:], AF.Silu, scale=gs_c[:, j:j + 1]), inc="cac")
            ring_evac(piA, sil_ev)
            c.wait_evt("vector", sil_ev)
            c.wait_evt("vector", umm)
            hd_ev = c.op("vector", lambda e, piB=piB, j=j: e.scalar_tensor_tensor(
                hdn[:, j * TG:(j + 1) * TG], ps_gemm[piB][:], us_c[:, j:j + 1], tsil[:],
                ALU.mult, ALU.mult), inc="cdv")
            ring_evac(piB, hd_ev)
            hdn_last = hd_ev

        c.phase(f"g{g}:f2ln4")
        f2_evts = []
        for m in range(16):
            bb, ldx = load_strip(scr_f2[m][:, :], 5632, gate=(prep_all if g == 0 else None))
            pi, wev = ring_next()
            c.wait_evt("tensor", wev)
            c.wait_evt("tensor", ldx)
            c.wait_evt("tensor", hdn_last)
            mm_ev = None
            for j in range(NJF):
                mm_ev = c.op("tensor", lambda e, pi=pi, j=j, bb=bb: e.matmul(
                    ps_gemm[pi][:], wsb[bb][:, j * 128:(j + 1) * 128],
                    hdn[:, j * TG:(j + 1) * TG], start=(j == 0), stop=(j == NJF - 1)),
                    inc=("cpe" if j == NJF - 1 else None))
            strip["use"][bb] = mm_ev
            c.wait_evt("scalar", mm_ev)
            f2_ev = c.op("scalar", lambda e, pi=pi, m=m: e.activation(
                ffn_fm[:, m * TG:(m + 1) * TG], ps_gemm[pi][:], AF.Copy,
                scale=ds_c[:, m:m + 1]), inc="cac")
            f2_evts.append(f2_ev)
            ring_evac(pi, f2_ev)

        ft_evts = []
        for m in range(16):
            pi, wev = ring_next()
            c.wait_evt("tensor", wev)
            c.wait_evt("tensor", f2_evts[m])
            tp_ev = None
            for t in range(NTI):
                tp_ev = c.op("tensor", lambda e, pi=pi, t=t, m=m: e.transpose(
                    ps_gemm_bf[pi][:, t * 128:(t + 1) * 128],
                    ffn_fm[:, m * TG + t * 128:m * TG + (t + 1) * 128], ident_bf[:]),
                    inc=("cpe" if t == 3 else None))
            c.wait_evt("vector", tp_ev)
            ft_ev = c.op("vector", lambda e, pi=pi, m=m: e.tensor_copy(
                mo_tm[:].rearrange("p (t q) -> p t q", q=D)[:, :, m * 128:(m + 1) * 128],
                ps_gemm_bf[pi][:, 0:512].rearrange("p (t q) -> p t q", q=128)), inc="cdv")
            ft_evts.append(ft_ev)
            ring_evac(pi, ft_ev)

        st_evts = []
        x2_prev = None
        for t in range(NTI):
            mt = mo_tm[:, t * D:(t + 1) * D]
            c.wait_evt("scalar", x2_prev)
            rstd_ev = ln_stats(mt, t, nm_scr[:].bitcast(BF16)[:, 0:D], src_evt=ft_evts[-1])
            c.wait_evt("vector", rstd_ev)
            c.op("vector", lambda e, mt=mt, t=t: e.tensor_scalar(
                nm_scr[:], mt, stat[:, 0 + t:1 + t], stat[:, 4 + t:5 + t],
                ALU.subtract, ALU.mult))
            if t >= 2:
                c.wait("vector", "qgp", st_evts[t - 2][1])
            x2_ev = c.op("vector", lambda e, t=t: e.tensor_tensor(
                x2_scr[:, (t % 2) * D:(t % 2 + 1) * D], nm_scr[:],
                xreg[:, t * D:(t + 1) * D], ALU.add), inc="cdv", drain=True)
            x2_prev = x2_ev
            c.wait_evt("gpsimd", x2_ev)
            tok0 = tok_base + t * 128
            st_evts.append(c.dma("gpsimd", out[seq, tok0:tok0 + 128, :],
                                 x2_scr[:, (t % 2) * D:(t % 2 + 1) * D], "qgp"))
        prev_group_store[0] = st_evts[-1]

        for (tname, kind, tg) in DEBUG_TAPS:
            if tg != g:
                continue
            tap = taps[tname]
            if kind == "x1":
                c.wait_evt("gpsimd", x1_evts[-1])
                c.dma("gpsimd", tap, xreg[:], "qgp")
            elif kind == "xn_fm":
                pass
            elif kind == "scr_ip":
                c.wait("gpsimd", prep_ip[0], prep_ip[1])
                for _s in range(34):
                    c.dma("gpsimd", tap[_s], scr_ip[_s], "qgp")
            elif kind == "acs":
                c.wait_evt("gpsimd", acs_evts[-1])
                c.wait_evt("gpsimd", lndt_ev)
                c.dma("gpsimd", tap[:, 0:TG], acs_fm[:], "qgp")
                c.dma("gpsimd", tap[:, TG:2 * TG], lndt_fm[:], "qgp")
                c.dma("gpsimd", tap[:, 2 * TG:3 * TG], dt_t2[:], "qgp")

    c.wait("gpsimd", "qgp", c.cnt["qgp"])

    with nc.Block() as block:
        def run(qn):
            def body(e):
                for fn in c.q[qn]:
                    fn(e)
            return body
        block.sync(run("sync"))
        block.tensor(run("tensor"))
        block.vector(run("vector"))
        block.scalar(run("scalar"))
        block.gpsimd(run("gpsimd"))

    for cm in c.sem_cm:
        cm.__exit__(None, None, None)
    build_program.pe_log = c.pe_log
    return nc


def _mk_shared(ips):
    (w_ip, conv_w, conv_b, dt_bias, A_log, Dv, norm_w, w_op,
     gate_w, up_w, down_w, gate_s, up_s, down_s) = ips
    f32 = lambda a: np.ascontiguousarray(np.asarray(a, dtype=np.float32))

    def tile_t(wt, nk, nm, mw=128):
        return np.ascontiguousarray(
            wt.reshape(nk, 128, nm, mw).transpose(2, 1, 0, 3).reshape(nm, 128, nk * mw))

    wip_t = f32(w_ip).T
    wl_ip = np.zeros((34, 128, 4096), np.float32)
    wl_ip[18:34] = tile_t(np.ascontiguousarray(wip_t[:, 0:4096]), 16, 16, 256)
    wl_ip[2:18] = tile_t(np.ascontiguousarray(wip_t[:, 4096:8192]), 16, 16, 256)
    wl_ip[1] = tile_t(np.ascontiguousarray(wip_t[:, 8192:8448]), 16, 1, 256)[0]
    dt_part = np.ascontiguousarray(wip_t[:, 8448:8512]).reshape(16, 128, 64)
    for k in range(16):
        wl_ip[0, :, k * 256:k * 256 + 64] = dt_part[k]
    wl_m4 = tile_t(f32(w_op).T, 32, 16)
    f1g = tile_t(f32(gate_w).T, 16, 44)
    f1u = tile_t(f32(up_w).T, 16, 44)
    wl_f1 = np.ascontiguousarray(np.concatenate([f1g, f1u], axis=2))
    wl_f2 = tile_t(f32(down_w).T, 44, 16)
    cw = f32(conv_w)
    wl_cv = np.zeros((34, 128, 512), np.float32)
    pidx = np.arange(128)
    conv_order = [32, 33] + list(range(32))
    for oi, cid in enumerate(conv_order):
        for k in range(4):
            wl_cv[oi, pidx, k * 128 + pidx] = cw[cid * 128 + pidx, k]
    return {
        "wl_ip": wl_ip, "wl_m4": wl_m4, "wl_f1": wl_f1, "wl_f2": wl_f2, "wl_cv": wl_cv,
        "conv_bl": f32(np.asarray(conv_b, dtype=np.float32).reshape(34, 128).T),
        "a_log": f32(np.asarray(A_log, dtype=np.float32).reshape(64, 1)),
        "dt_bias": f32(np.asarray(dt_bias, dtype=np.float32).reshape(64, 1)),
        "d_bc": f32(np.broadcast_to(np.asarray(Dv, dtype=np.float32).reshape(1, 64),
                                    (128, 64))),
        "norm_wl": f32(np.asarray(norm_w, dtype=np.float32).reshape(32, 128).T),
        "gs_l": f32(np.asarray(gate_s, dtype=np.float32).reshape(44, 128).T),
        "us_l": f32(np.asarray(up_s, dtype=np.float32).reshape(44, 128).T),
        "ds_l": f32(np.asarray(down_s, dtype=np.float32).reshape(16, 128).T),
        "ident_bf": _bf(np.eye(128)),
        "ident_f": np.eye(128, dtype=np.float32),
        "trimask": np.triu(np.ones((128, 128), dtype=np.float32)),
        "ones_t": np.ones((128, 128), dtype=np.float32),
        "ones_r": np.ones((1, 128), dtype=np.float32),
        "misc_l": np.ascontiguousarray(
            np.broadcast_to(np.array([1e-5, 1.0], dtype=np.float32), (128, 2))),
    }


def kernel(L0_attn_hidden_states, L0_attn_in_proj_w, L0_attn_conv_w, L0_attn_conv_b,
           L0_attn_dt_bias, L0_attn_A_log, L0_attn_D, L0_attn_norm_w, L0_attn_out_proj_w,
           L0_ffn_gate_w, L0_ffn_up_w, L0_ffn_down_w, L0_ffn_gate_s, L0_ffn_up_s,
           L0_ffn_down_s):
    x = np.asarray(L0_attn_hidden_states, dtype=np.float32)
    shared = _mk_shared((L0_attn_in_proj_w, L0_attn_conv_w, L0_attn_conv_b,
                         L0_attn_dt_bias, L0_attn_A_log, L0_attn_D, L0_attn_norm_w,
                         L0_attn_out_proj_w, L0_ffn_gate_w, L0_ffn_up_w, L0_ffn_down_w,
                         L0_ffn_gate_s, L0_ffn_up_s, L0_ffn_down_s))
    nc = build_program()
    in_maps = []
    for i in range(NCORE):
        m = dict(shared)
        m["x"] = np.ascontiguousarray(x[2 * i:2 * i + 2])
        in_maps.append(m)
    res = run_bass_kernel_spmd(nc, in_maps, core_ids=list(range(NCORE)))
    return np.concatenate([r["o"] for r in res.results], axis=0)
